# version 14
# baseline (speedup 1.0000x reference)
import os
os.environ.setdefault("TRN_ALLOW_ICI_ALL_TO_ALL", "1")
import numpy as np
import ml_dtypes
from contextlib import ExitStack

import concourse.bass as bass
import concourse.mybir as mybir
from concourse.bass_utils import run_bass_kernel_spmd

F32 = mybir.dt.float32
BF16 = mybir.dt.bfloat16
U32 = mybir.dt.uint32
AF = mybir.ActivationFunctionType
ALU = mybir.AluOpType
AX = mybir.AxisListType

NCORE = 8
TT = 16384
D = 2048
DST = 16
DTR = 128
XDB = 160
NEXP = 32
FEXP = 768
DSL = 256
NT = 128
OWNT = 16
KT = 16
MTOK = 1024
NFC = 6
NE_ALL = 34



class Res:
    __slots__ = ("name", "lw", "readers")

    def __init__(self, name):
        self.name = name
        self.lw = {}
        self.readers = {}


class EngS:
    __slots__ = ("name", "sems", "counts", "idx", "waited", "items", "host",
                 "pend_reads", "pend_writes")

    def __init__(self, name, sems):
        self.name = name
        self.sems = sems
        self.counts = [0] * len(sems)
        self.idx = 0
        self.waited = {}
        self.items = []
        self.host = None
        self.pend_reads = []
        self.pend_writes = []


def _merge(d, sem, val):
    if d.get(sem, 0) < val:
        d[sem] = val


class Sched:
    def __init__(self, nc, stack):
        self.nc = nc
        self.stack = stack
        self.eng = {}
        self.queues = {}

    def add_engine(self, name, nsem):
        sems = [self.stack.enter_context(self.nc.semaphore(f"es_{name}{i}"))
                for i in range(nsem)]
        self.eng[name] = EngS(name, sems)

    def add_queue(self, name, ename, nsem):
        sems = [self.stack.enter_context(self.nc.semaphore(f"qs_{name}{i}"))
                for i in range(nsem)]
        q = EngS(name, sems)
        q.host = ename
        self.queues[name] = q

    def _waits(self, e, reads, writes):
        need = {}
        for r in reads:
            for sem, val in r.lw.items():
                _merge(need, sem, val)
        for w in writes:
            for sem, val in w.lw.items():
                _merge(need, sem, val)
            for sem, val in w.readers.items():
                _merge(need, sem, val)
        out = []
        for sem, val in need.items():
            if e.waited.get(sem, 0) < val:
                out.append((sem, val))
            _merge(e.waited, sem, val)
        return out

    def _finish(self, e, tok, reads, writes):
        if tok is None:
            e.pend_reads.extend(reads)
            e.pend_writes.extend(writes)
            return None
        sem, val = tok
        for r in e.pend_reads:
            _merge(r.readers, sem, val)
        for w in e.pend_writes:
            _merge(w.lw, sem, val)
            w.readers = {}
        e.pend_reads = []
        e.pend_writes = []
        for r in reads:
            _merge(r.readers, sem, val)
        for w in writes:
            _merge(w.lw, sem, val)
            w.readers = {}
        return tok

    def op(self, ename, fn, reads=(), writes=(), inc=True):
        e = self.eng[ename]
        waits = self._waits(e, reads, writes)
        tok = None
        if inc:
            i = e.idx
            e.idx = (e.idx + 1) % len(e.sems)
            e.counts[i] += 1
            tok = (e.sems[i], e.counts[i])
        e.items.append((waits, fn, tok, 1))
        return self._finish(e, tok, reads, writes)

    def dma(self, qname, fn, reads=(), writes=()):
        q = self.queues[qname]
        e = self.eng[q.host]
        waits = self._waits(e, reads, writes)
        i = q.idx
        q.idx = (q.idx + 1) % len(q.sems)
        q.counts[i] += 16
        tok = (q.sems[i], q.counts[i])
        e.items.append((waits, fn, tok, 16))
        return self._finish(e, tok, reads, writes)

    def wait_tokens(self, ename, toks):
        e = self.eng[ename]
        need = {}
        for sem, val in toks:
            _merge(need, sem, val)
        waits = []
        for sem, val in need.items():
            if e.waited.get(sem, 0) < val:
                waits.append((sem, val))
            _merge(e.waited, sem, val)
        if waits:
            e.items.append((waits, None, None, 0))

    def emit(self, block):
        def mk(e):
            def run(eng):
                for waits, fn, tok, amt in e.items:
                    for sem, val in waits:
                        eng.wait_ge(sem, val)
                    if fn is not None:
                        inst = fn(eng)
                        if tok is not None:
                            inst.then_inc(tok[0], amt)
            return run
        block.sync(mk(self.eng["sync"]))
        block.vector(mk(self.eng["vector"]))
        block.scalar(mk(self.eng["scalar"]))
        block.tensor(mk(self.eng["tensor"]))
        block.gpsimd(mk(self.eng["gpsimd"]))



def build_program():
    nc = bass.Bass("TRN2", target_bir_lowering=False, debug=False)

    def din(name, shape, dtype=F32):
        return nc.dram_tensor(name, list(shape), dtype, kind="ExternalInput").ap()

    x_all = din("x_all", [TT, D])
    x_sl = din("x_sl", [TT, DSL])
    x_own = din("x_own", [2048, D])
    alog_sl = din("alog_sl", [DSL, DST])
    d_sl = din("d_sl", [DSL, 1])
    xpwT_in = din("xpwT", [D, XDB])
    dtwT_in = din("dtwT_sl", [DTR, DSL])
    bias_in = din("bias_sl", [DSL, 1])
    gate_in = din("gate_w", [D, NEXP])
    ghi_in = din("ghi", [NE_ALL, NFC, 128, D], BF16)
    uhi_in = din("uhi", [NE_ALL, NFC, 128, D], BF16)
    dhi_in = din("dhi", [NE_ALL, NFC, 128, D], BF16)
    glo_in = din("glo", [NE_ALL, NFC, 128, D], BF16)
    ulo_in = din("ulo", [NE_ALL, NFC, 128, D], BF16)
    dlo_in = din("dlo", [NE_ALL, NFC, 128, D], BF16)
    idf_in = din("id_f32", [128, 128], F32)

    out_own = nc.dram_tensor("out_own", [2048, D], F32, kind="ExternalOutput").ap()

    dT_dram = nc.dram_tensor("dT_dram", [DSL, TT], F32, kind="Internal").ap()
    uT_dram = nc.dram_tensor("uT_dram", [DSL, TT], F32, kind="Internal").ap()
    Bflat = nc.dram_tensor("Bflat", [TT, DST], F32, kind="Internal").ap()
    Cflat = nc.dram_tensor("Cflat", [TT, DST], F32, kind="Internal").ap()
    xnTB_dram = nc.dram_tensor("xnTB_dram", [D, MTOK], BF16, kind="Internal").ap()
    accA_dram = nc.dram_tensor("accA_dram", [D, MTOK], F32, kind="Internal").ap()
    combT_dram = nc.dram_tensor("combT_dram", [NEXP, 2048], F32, kind="Internal").ap()
    ag_in = nc.dram_tensor("ag_in", [D, 2048], F32, kind="Internal").ap()
    ag_out = nc.dram_tensor("ag_out", [D, 2048], F32, kind="Internal").ap()

    stack = ExitStack()
    E = stack.enter_context

    sbufs = {}

    def sbuf(name, shape, dtype):
        t = E(nc.sbuf_tensor(name, list(shape), dtype))
        sbufs[name] = t
        return t

    xnT_own = [sbuf(f"xnT_own{k}", [128, MTOK], BF16) for k in range(KT)]
    acc = [sbuf(f"acc{j}", [128, MTOK], F32) for j in range(KT)]
    cmb = sbuf("cmb", [128, OWNT * NEXP], F32)
    c_rep = sbuf("c_rep", [128, MTOK], F32)
    wg_bf = [sbuf(f"wg_bf{p}", [128, D], BF16) for p in range(2)]
    wu_bf = [sbuf(f"wu_bf{p}", [128, D], BF16) for p in range(2)]
    wd_bf = [sbuf(f"wd_bf{f}", [128, D], BF16) for f in range(NFC)]
    x_half = sbuf("x_half", [128, 1024], F32)
    xsl_tile = sbuf("xsl_tile", [128, DSL], F32)
    xn_bf = sbuf("xn_bf", [128, D], BF16)
    xnT_stage = sbuf("xnT_stage", [128, D], BF16)
    xbt = sbuf("xbt", [128, 256], BF16)
    xnT_f32s = sbuf("xnT_f32s", [128, 128], F32)
    dpre_sb = sbuf("dpre_sb", [128, DSL], F32)
    sp_t1 = sbuf("sp_t1", [128, DSL], F32)
    xn_sl = [sbuf(f"xn_sl{p}", [128, DSL], F32) for p in range(2)]
    tpose_f32 = sbuf("tpose_f32", [128, 128], F32)
    stat = sbuf("stat", [128, 16], F32)
    xdbl_s = sbuf("xdbl_s", [128, XDB], F32)
    sc_sb = sbuf("sc_sb", [128, NEXP], F32)
    m8_sb = sbuf("m8_sb", [128, 8], F32)
    msk_sb = sbuf("msk_sb", [128, NEXP], F32)
    den_sb = sbuf("den_sb", [128, 2], F32)
    dT_w = sbuf("dT_w", [128, 128], F32)
    uT_w = sbuf("uT_w", [128, 128], F32)
    du_w = sbuf("du_w", [128, 128], F32)
    B_rep = sbuf("B_rep", [128, 128 * DST], F32)
    C_rep = sbuf("C_rep", [128, 128 * DST], F32)
    dA_buf = sbuf("dA_buf", [128, 1024], F32)
    h_buf = sbuf("h_buf", [128, 1024], F32)
    carry = [sbuf(f"carry{t}", [128, DST], F32) for t in range(2)]
    y_h = sbuf("y_h", [128, 128], F32)
    A_sb = [sbuf(f"A_sb{t}", [128, DST], F32) for t in range(2)]
    D_sb = [sbuf(f"D_sb{t}", [128, 1], F32) for t in range(2)]
    silu_g = sbuf("silu_g", [128, MTOK], BF16)
    act_f = [sbuf(f"act_f{f}", [128, MTOK], BF16) for f in range(NFC)]
    sid_f32 = sbuf("sid_f32", [128, 128], F32)
    xpwT_bf = sbuf("xpwT_bf", [128, KT * XDB], BF16)
    dtwT_bf = sbuf("dtwT_bf", [128, DSL], BF16)
    bias2 = sbuf("bias2", [128, 6], F32)
    bias_sb = [bias2[:, 0:1], bias2[:, 1:2]]
    sconst = bias2[:, 2:6]

    s_out = dA_buf[:, :]
    gate_sb = c_rep[:, 0:512]
    xn_f32q = dA_buf[:, 0:512]
    stg_small = h_buf
    alog_sb = h_buf[:, 512:544]
    y_in = [cmb[:, 0:128], cmb[:, 128:256]]
    acc_in = [cmb[:, 256:384], cmb[:, 384:512]]
    ctT_stage = tpose_f32[0:NEXP, :]
    dr_bf = xbt[:, 0:128]
    drT_bf = xbt[:, 128:256]

    pg = E(nc.psum_tensor("pg", [128, MTOK], F32))
    pu = E(nc.psum_tensor("pu", [128, MTOK], F32))
    pr = E(nc.psum_tensor("pr", [128, MTOK], F32))
    pm = E(nc.psum_tensor("pm", [128, 512], F32))
    prx = E(nc.psum_tensor("prx", [128, 512], F32))

    S = Sched(nc, stack)
    for en, ns in (("sync", 2), ("vector", 2), ("scalar", 2), ("tensor", 2),
                   ("gpsimd", 2)):
        S.add_engine(en, ns)
    S.add_queue("qx", "sync", 4)
    S.add_queue("qw", "sync", 8)
    S.add_queue("qp", "scalar", 4)
    S.add_queue("qs", "scalar", 4)
    S.add_queue("qt", "scalar", 4)
    S.add_queue("qv", "scalar", 4)

    R = {name: Res(name) for name in sbufs}
    for extra in ("pg", "pu", "pr", "pr_hi", "prx", "pm", "xnTB", "accA", "ag_out",
                  "combT_dram", "ext_in", "out_dram",
                  "y_in0", "y_in1", "acc_in0", "acc_in1"):
        R[extra] = Res(extra)
    prep_res = [Res(f"prep{w}") for w in range(NT)]
    yw_res = [Res(f"yw{w}") for w in range(NT)]
    rr = R.__getitem__

    def cast_bf16(engine, dst_ap, src_ap, reads, writes):
        if engine == "vector":
            S.op("vector", lambda e, d=dst_ap, s=src_ap: e.tensor_copy(out=d, in_=s),
                 reads=reads, writes=writes)
        else:
            S.op("scalar", lambda e, d=dst_ap, s=src_ap: e.copy(out=d, in_=s),
                 reads=reads, writes=writes)

    def emit_startup():
        for fn, w in (
            ((lambda e: e.dma_start(out=sid_f32[:, :], in_=idf_in)), "sid_f32"),
            ((lambda e: e.dma_start(out=alog_sb[:, 0:DST], in_=alog_sl[0:128, :])), "h_buf"),
            ((lambda e: e.dma_start(out=alog_sb[:, DST:2 * DST], in_=alog_sl[128:256, :])), "h_buf"),
            ((lambda e: e.dma_start(out=D_sb[0][:, :], in_=d_sl[0:128, :])), "D_sb0"),
            ((lambda e: e.dma_start(out=D_sb[1][:, :], in_=d_sl[128:256, :])), "D_sb1"),
            ((lambda e: e.dma_start(out=bias_sb[0], in_=bias_in[0:128, :])), "bias2"),
            ((lambda e: e.dma_start(out=bias_sb[1], in_=bias_in[128:256, :])), "bias2"),
        ):
            S.dma("qx", fn, reads=[rr("ext_in")], writes=[rr(w)])
        for ci, cv in ((0, 1e-5), (1, -0.5), (2, -1.0), (3, 1.0)):
            S.op("vector", lambda e, ci=ci, cv=cv: e.memset(sconst[:, ci:ci + 1], cv),
                 reads=[], writes=[rr("bias2")])
        gk = gate_in.rearrange("(k p) e -> k p e", p=128)
        S.dma("qx", lambda e: e.dma_start(
                  out=gate_sb.rearrange("p (k e) -> p k e", k=KT),
                  in_=gk.transpose([1, 0, 2])),
              reads=[rr("ext_in")], writes=[rr("c_rep")])
        S.op("scalar", lambda e: e.activation(out=alog_sb[:, :], in_=alog_sb[:, :],
                                              func=AF.Exp),
             reads=[rr("h_buf")], writes=[rr("h_buf")])
        for t in range(2):
            S.op("vector", lambda e, t=t: e.tensor_scalar_mul(
                     out=A_sb[t][:, :], in0=alog_sb[:, DST * t:DST * (t + 1)],
                     scalar1=-1.0),
                 reads=[rr("h_buf")], writes=[rr(f"A_sb{t}")])
        xk = xpwT_in.rearrange("(k p) e -> k p e", p=128)
        for k in range(KT):
            S.dma("qx", lambda e, k=k: e.dma_start(out=stg_small[:, 0:XDB],
                                                   in_=xk[k, :, :]),
                  reads=[rr("ext_in")], writes=[rr("h_buf")])
            cast_bf16("vector", xpwT_bf[:, XDB * k:XDB * (k + 1)],
                      stg_small[:, 0:XDB], [rr("h_buf")], [rr("xpwT_bf")])
        S.dma("qx", lambda e: e.dma_start(out=stg_small[:, 0:DSL], in_=dtwT_in),
              reads=[rr("ext_in")], writes=[rr("h_buf")])
        cast_bf16("vector", dtwT_bf[:, :], stg_small[:, 0:DSL], [rr("h_buf")],
                  [rr("dtwT_bf")])

    def emit_stats_xn(src_rows_fn, f32_halves, so=0):
        st = stat[:, :]
        for h in range(2):
            S.dma("qx", lambda e, h=h: e.dma_start(out=x_half[:, :],
                                                   in_=src_rows_fn(h)),
                  reads=[rr("ext_in")], writes=[rr("x_half")])
            S.op("vector", lambda e, h=h: e.tensor_reduce(
                     out=st[:, (so if h == 0 else so + 2):(so + 1 if h == 0 else so + 3)],
                     in_=x_half[:, :], axis=AX.X, op=ALU.add),
                 reads=[rr("x_half")], writes=[rr("stat")])
            S.op("vector", lambda e, h=h: e.scalar_tensor_tensor(
                     out=xn_bf[:, 1024 * h:1024 * (h + 1)], in0=x_half[:, :],
                     scalar=1.0, in1=x_half[:, :], op0=ALU.mult, op1=ALU.mult,
                     accum_out=st[:, (so + 1 if h == 0 else so + 3):(so + 2 if h == 0 else so + 4)]),
                 reads=[rr("x_half")], writes=[rr("xn_bf"), rr("stat")])
            if h == 0:
                continue
            S.op("vector", lambda e: e.tensor_tensor(out=st[:, so + 0:so + 1], in0=st[:, so + 0:so + 1],
                                                     in1=st[:, so + 2:so + 3], op=ALU.add),
                 reads=[rr("stat")], writes=[rr("stat")])
            S.op("vector", lambda e: e.tensor_tensor(out=st[:, so + 1:so + 2], in0=st[:, so + 1:so + 2],
                                                     in1=st[:, so + 3:so + 4], op=ALU.add),
                 reads=[rr("stat")], writes=[rr("stat")])
            S.op("vector", lambda e: e.tensor_scalar_mul(out=st[:, so + 2:so + 3],
                                                         in0=st[:, so + 0:so + 1],
                                                         scalar1=1.0 / D),
                 reads=[rr("stat")], writes=[rr("stat")])
            S.op("vector", lambda e: e.tensor_tensor(out=st[:, so + 3:so + 4], in0=st[:, so + 2:so + 3],
                                                     in1=st[:, so + 2:so + 3], op=ALU.mult),
                 reads=[rr("stat")], writes=[rr("stat")])
            S.op("vector", lambda e: e.scalar_tensor_tensor(
                     out=st[:, so + 4:so + 5], in0=st[:, so + 1:so + 2], scalar=1.0 / D, in1=st[:, so + 3:so + 4],
                     op0=ALU.mult, op1=ALU.subtract),
                 reads=[rr("stat")], writes=[rr("stat")])
            S.op("scalar", lambda e: e.activation(out=st[:, so + 4:so + 5], in_=st[:, so + 4:so + 5],
                                                  func=AF.Ln, bias=sconst[:, 0:1]),
                 reads=[rr("stat"), rr("bias2")], writes=[rr("stat")])
            S.op("scalar", lambda e: e.activation(out=st[:, so + 5:so + 6], in_=st[:, so + 4:so + 5],
                                                  func=AF.Exp, scale=sconst[:, 1:2]),
                 reads=[rr("stat"), rr("bias2")], writes=[rr("stat")])
            S.op("vector", lambda e: e.scalar_tensor_tensor(
                     out=st[:, so + 6:so + 7], in0=st[:, so + 2:so + 3], scalar=-1.0, in1=st[:, so + 5:so + 6],
                     op0=ALU.mult, op1=ALU.mult),
                 reads=[rr("stat")], writes=[rr("stat")])
        for h in range(2):
            S.dma("qx", lambda e, h=h: e.dma_start(out=x_half[:, :],
                                                   in_=src_rows_fn(h)),
                  reads=[rr("ext_in")], writes=[rr("x_half")])
            S.op("scalar", lambda e, h=h: e.activation(
                     out=xn_bf[:, 1024 * h:1024 * (h + 1)], in_=x_half[:, :],
                     func=AF.Identity, scale=st[:, so + 5:so + 6], bias=st[:, so + 6:so + 7]),
                 reads=[rr("x_half"), rr("stat")], writes=[rr("xn_bf")])
            if f32_halves:
                for q2 in range(2):
                    S.op("scalar", lambda e, q2=q2: e.activation(
                             out=xn_f32q, in_=x_half[:, 512 * q2:512 * (q2 + 1)],
                             func=AF.Identity, scale=st[:, so + 5:so + 6], bias=st[:, so + 6:so + 7]),
                         reads=[rr("x_half"), rr("stat")], writes=[rr("dA_buf")])
                    yield 2 * h + q2

    def xnT_k(k, dst_ap, dst_res):
        S.dma("qt", lambda e, k=k, ap=dst_ap: e.dma_start_transpose(
                  out=ap, in_=xn_bf[:, 128 * k:128 * (k + 1)]),
              reads=[rr("xn_bf")], writes=[dst_res])

    def emit_miniprep(j):
        src = lambda h: x_own[128 * j:128 * (j + 1), 1024 * h:1024 * (h + 1)]
        for q in emit_stats_xn(src, True):
            for kk in range(4):
                k = 4 * q + kk
                reg = kk % 2
                pslc = pm[:, 256 + 128 * reg:256 + 128 * (reg + 1)]
                S.op("tensor", lambda e, kk=kk, p=pslc: e.transpose(
                         p, in_=xn_f32q[:, 128 * kk:128 * (kk + 1)],
                         identity=sid_f32[:, :]),
                     reads=[rr("dA_buf"), rr("sid_f32")], writes=[rr("pm")])
                S.op("vector", lambda e, p=pslc: e.tensor_copy(
                         out=xnT_f32s[:, :], in_=p),
                     reads=[rr("pm")], writes=[rr("xnT_f32s")])
                S.op("tensor", lambda e, k=k: e.matmul(
                         pr[:, 0:NEXP], lhsT=xnT_f32s[:, :],
                         rhs=gate_sb[:, NEXP * k:NEXP * (k + 1)],
                         start=(k == 0), stop=(k == KT - 1)),
                     reads=[rr("xnT_f32s"), rr("c_rep")], writes=[rr("pr")],
                     inc=(k == KT - 1))
        if j < 8:
            for k in range(KT):
                xnT_k(k, xnT_own[k][:, 128 * j:128 * (j + 1)], rr(f"xnT_own{k}"))
        else:
            jj = j - 8
            for k in range(KT):
                par = k % 2
                xnT_k(k, xbt[:, 128 * par:128 * (par + 1)], rr("xbt"))
                S.dma("qx", lambda e, k=k, jj=jj, par=par: e.dma_start(
                          out=xnTB_dram[128 * k:128 * (k + 1),
                                        128 * jj:128 * (jj + 1)],
                          in_=xbt[:, 128 * par:128 * (par + 1)]),
                      reads=[rr("xbt")], writes=[rr("xnTB")])
        S.op("vector", lambda e: e.tensor_copy(out=sc_sb[:, :], in_=pr[:, 0:NEXP]),
             reads=[rr("pr")], writes=[rr("sc_sb")])
        S.op("scalar", lambda e: e.activation(out=sc_sb[:, :], in_=sc_sb[:, :],
                                              func=AF.Exp, scale=sconst[:, 2:3]),
             reads=[rr("sc_sb"), rr("bias2")], writes=[rr("sc_sb")])
        S.op("vector", lambda e: e.tensor_scalar_add(out=sc_sb[:, :],
                                                     in0=sc_sb[:, :], scalar1=1.0),
             reads=[rr("sc_sb")], writes=[rr("sc_sb")])
        S.op("vector", lambda e: e.reciprocal(out=sc_sb[:, :], in_=sc_sb[:, :]),
             reads=[rr("sc_sb")], writes=[rr("sc_sb")])
        S.op("vector", lambda e: e.max(out=m8_sb[:, :], in_=sc_sb[:, :]),
             reads=[rr("sc_sb")], writes=[rr("m8_sb")])
        S.op("vector", lambda e: e.tensor_single_scalar(
                 out=msk_sb[:, :], in_=sc_sb[:, :], scalar=m8_sb[:, 5:6],
                 op=ALU.is_ge),
             reads=[rr("sc_sb"), rr("m8_sb")], writes=[rr("msk_sb")])
        S.op("vector", lambda e: e.tensor_reduce(out=den_sb[:, 0:1],
                                                 in_=m8_sb[:, 0:6], axis=AX.X,
                                                 op=ALU.add),
             reads=[rr("m8_sb")], writes=[rr("den_sb")])
        S.op("vector", lambda e: e.tensor_scalar_add(out=den_sb[:, 0:1],
                                                     in0=den_sb[:, 0:1],
                                                     scalar1=1e-20),
             reads=[rr("den_sb")], writes=[rr("den_sb")])
        S.op("vector", lambda e: e.reciprocal(out=den_sb[:, 1:2],
                                              in_=den_sb[:, 0:1]),
             reads=[rr("den_sb")], writes=[rr("den_sb")])
        S.op("vector", lambda e: e.tensor_tensor(out=msk_sb[:, :], in0=msk_sb[:, :],
                                                 in1=sc_sb[:, :], op=ALU.mult),
             reads=[rr("msk_sb"), rr("sc_sb")], writes=[rr("msk_sb")])
        cslc = cmb[:, NEXP * j:NEXP * (j + 1)]
        S.op("vector", lambda e, c=cslc: e.tensor_scalar_mul(
                 out=c, in0=msk_sb[:, :], scalar1=den_sb[:, 1:2]),
             reads=[rr("msk_sb"), rr("den_sb")], writes=[rr("cmb")])
        S.op("tensor", lambda e, c=cslc: e.transpose(
                 pm[0:NEXP, 256:384], in_=c, identity=sid_f32[:, :]),
             reads=[rr("cmb"), rr("sid_f32")], writes=[rr("pm")])
        S.op("vector", lambda e: e.tensor_copy(out=ctT_stage[:, :],
                                               in_=pm[0:NEXP, 256:384]),
             reads=[rr("pm")], writes=[rr("tpose_f32")])
        S.dma("qp", lambda e, j=j: e.dma_start(
                  out=combT_dram[:, 128 * j:128 * (j + 1)], in_=ctT_stage[:, :]),
              reads=[rr("tpose_f32")], writes=[rr("combT_dram")])

    def emit_prep(i):
        so = 8 * (i % 2)
        xp = i % 2
        src = lambda h: x_all[128 * i:128 * (i + 1), 1024 * h:1024 * (h + 1)]
        for _ in emit_stats_xn(src, False, so):
            pass
        S.dma("qx", lambda e, i=i: e.dma_start(
                  out=xsl_tile[:, :], in_=x_sl[128 * i:128 * (i + 1), :]),
              reads=[rr("ext_in")], writes=[rr("xsl_tile")])
        S.op("scalar", lambda e, so=so, xp=xp: e.activation(
                 out=xn_sl[xp][:, :], in_=xsl_tile[:, :],
                 func=AF.Identity, scale=stat[:, so + 5:so + 6],
                 bias=stat[:, so + 6:so + 7]),
             reads=[rr("xsl_tile"), rr("stat")], writes=[rr(f"xn_sl{xp}")])
        for k in range(KT):
            xnT_k(k, xnT_stage[:, 128 * k:128 * (k + 1)], rr("xnT_stage"))
        yield
        yield
        for k in range(KT):
            S.op("tensor", lambda e, k=k: e.matmul(
                     pm[:, 0:XDB], lhsT=xnT_stage[:, 128 * k:128 * (k + 1)],
                     rhs=xpwT_bf[:, XDB * k:XDB * (k + 1)],
                     start=(k == 0), stop=(k == KT - 1)),
                 reads=[rr("xnT_stage"), rr("xpwT_bf")], writes=[rr("pm")],
                 inc=(k == KT - 1))
        S.op("scalar", lambda e: e.copy(out=xdbl_s[:, 128:160], in_=pm[:, 128:160]),
             reads=[rr("pm")], writes=[rr("xdbl_s")])
        S.dma("qp", lambda e, i=i: e.dma_start(out=Bflat[128 * i:128 * (i + 1), :],
                                               in_=xdbl_s[:, 128:144]),
              reads=[rr("xdbl_s")], writes=[prep_res[i]])
        S.dma("qp", lambda e, i=i: e.dma_start(out=Cflat[128 * i:128 * (i + 1), :],
                                               in_=xdbl_s[:, 144:160]),
              reads=[rr("xdbl_s")], writes=[prep_res[i]])
        S.op("scalar", lambda e: e.copy(out=dr_bf, in_=pm[:, 0:DTR]),
             reads=[rr("pm")], writes=[rr("xbt")])
        S.dma("qt", lambda e: e.dma_start_transpose(out=drT_bf, in_=dr_bf),
              reads=[rr("xbt")], writes=[rr("xbt")])
        yield
        for dt in range(2):
            S.op("tensor", lambda e, dt=dt: e.matmul(
                     pm[:, 128 * dt:128 * (dt + 1)],
                     lhsT=dtwT_bf[:, 128 * dt:128 * (dt + 1)],
                     rhs=drT_bf, start=True, stop=True),
                 reads=[rr("xbt"), rr("dtwT_bf")], writes=[rr("pm")])
        for dt in range(2):
            S.op("scalar", lambda e, dt=dt: e.activation(
                     out=sp_t1[:, 128 * dt:128 * (dt + 1)],
                     in_=pm[:, 128 * dt:128 * (dt + 1)],
                     func=AF.Abs, bias=bias_sb[dt]),
                 reads=[rr("pm"), rr("bias2")], writes=[rr("sp_t1")])
            S.op("scalar", lambda e, dt=dt: e.activation(
                     out=sp_t1[:, 128 * dt:128 * (dt + 1)],
                     in_=sp_t1[:, 128 * dt:128 * (dt + 1)],
                     func=AF.Exp, scale=sconst[:, 2:3]),
                 reads=[rr("sp_t1"), rr("bias2")], writes=[rr("sp_t1")])
            S.op("scalar", lambda e, dt=dt: e.activation(
                     out=sp_t1[:, 128 * dt:128 * (dt + 1)],
                     in_=sp_t1[:, 128 * dt:128 * (dt + 1)],
                     func=AF.Ln, bias=sconst[:, 3:4]),
                 reads=[rr("sp_t1"), rr("bias2")], writes=[rr("sp_t1")])
            S.op("scalar", lambda e, dt=dt: e.activation(
                     out=dpre_sb[:, 128 * dt:128 * (dt + 1)],
                     in_=pm[:, 128 * dt:128 * (dt + 1)],
                     func=AF.Relu, bias=bias_sb[dt]),
                 reads=[rr("pm"), rr("bias2")], writes=[rr("dpre_sb")])
        yield
        for dt in range(2):
            tb = tpose_f32 if dt == 0 else xnT_f32s
            tbr = rr("tpose_f32") if dt == 0 else rr("xnT_f32s")
            S.op("gpsimd", lambda e, dt=dt, tb=tb: e.tensor_tensor(
                     out=tb[:, :], in0=sp_t1[:, 128 * dt:128 * (dt + 1)],
                     in1=dpre_sb[:, 128 * dt:128 * (dt + 1)], op=ALU.add),
                 reads=[rr("sp_t1"), rr("dpre_sb")], writes=[tbr])
            S.dma("qp", lambda e, dt=dt, i=i, tb=tb: e.dma_start(
                      out=dT_dram[128 * dt:128 * (dt + 1), 128 * i:128 * (i + 1)],
                      in_=tb[:, :]),
                  reads=[tbr], writes=[prep_res[i]])
        yield
        for dt in range(2):
            pslc = pm[:, 256 + 128 * dt:256 + 128 * (dt + 1)]
            tb = tpose_f32 if dt == 0 else xnT_f32s
            tbr = rr("tpose_f32") if dt == 0 else rr("xnT_f32s")
            S.op("tensor", lambda e, dt=dt, p=pslc, xp=xp: e.transpose(
                     p, in_=xn_sl[xp][:, 128 * dt:128 * (dt + 1)],
                     identity=sid_f32[:, :]),
                 reads=[rr(f"xn_sl{xp}"), rr("sid_f32")], writes=[rr("pm")])
            S.op("scalar", lambda e, p=pslc, tb=tb: e.copy(out=tb[:, :], in_=p),
                 reads=[rr("pm")], writes=[tbr])
            S.dma("qp", lambda e, dt=dt, i=i, tb=tb: e.dma_start(
                      out=uT_dram[128 * dt:128 * (dt + 1), 128 * i:128 * (i + 1)],
                      in_=tb[:, :]),
                  reads=[tbr], writes=[prep_res[i]])

    def emit_scan_gen(w):
        S.dma("qs", lambda e, w=w: e.dma_start(
                  out=B_rep[:, :], in_=Bflat[128 * w:128 * (w + 1), :]
                      .rearrange("l n -> (l n)").unsqueeze(0)
                      .broadcast_to([128, 128 * DST])),
              reads=[prep_res[w]], writes=[rr("B_rep")])
        S.dma("qs", lambda e, w=w: e.dma_start(
                  out=C_rep[:, :], in_=Cflat[128 * w:128 * (w + 1), :]
                      .rearrange("l n -> (l n)").unsqueeze(0)
                      .broadcast_to([128, 128 * DST])),
              reads=[prep_res[w]], writes=[rr("C_rep")])
        B3f = B_rep[:, :].rearrange("p (l n) -> p l n", n=DST)
        C3f = C_rep[:, :].rearrange("p (l n) -> p l n", n=DST)
        dA3 = dA_buf[:, :].rearrange("p (l n) -> p l n", n=8)
        h3 = h_buf[:, :].rearrange("p (l n) -> p l n", n=8)
        yield
        for dt in range(2):
            S.dma("qs", lambda e, dt=dt, w=w: e.dma_start(
                      out=dT_w[:, :],
                      in_=dT_dram[128 * dt:128 * (dt + 1), 128 * w:128 * (w + 1)]),
                  reads=[prep_res[w]], writes=[rr("dT_w")])
            S.dma("qs", lambda e, dt=dt, w=w: e.dma_start(
                      out=uT_w[:, :],
                      in_=uT_dram[128 * dt:128 * (dt + 1), 128 * w:128 * (w + 1)]),
                  reads=[prep_res[w]], writes=[rr("uT_w")])
            yield
            S.op("vector", lambda e: e.tensor_tensor(
                     out=du_w[:, :], in0=dT_w[:, :], in1=uT_w[:, :], op=ALU.mult),
                 reads=[rr("dT_w"), rr("uT_w")], writes=[rr("du_w")])
            if w % 64 == 0:
                S.op("vector", lambda e, dt=dt: e.memset(carry[dt][:, :], 0.0),
                     reads=[], writes=[rr(f"carry{dt}")])
            for hf in range(2):
                n0 = 8 * hf
                dT3 = dT_w[:, :].unsqueeze(2).broadcast_to([128, 128, 8])
                A3 = A_sb[dt][:, n0:n0 + 8].unsqueeze(1).broadcast_to([128, 128, 8])
                S.op("gpsimd", lambda e, dT3=dT3, A3=A3: e.tensor_tensor(
                         out=dA3, in0=dT3, in1=A3, op=ALU.mult),
                     reads=[rr("dT_w"), rr(f"A_sb{dt}")], writes=[rr("dA_buf")])
                S.op("scalar", lambda e: e.activation(out=dA_buf[:, :],
                                                      in_=dA_buf[:, :], func=AF.Exp),
                     reads=[rr("dA_buf")], writes=[rr("dA_buf")])
                yield
                du3 = du_w[:, :].unsqueeze(2).broadcast_to([128, 128, 8])
                B3 = B3f[:, :, n0:n0 + 8]
                S.op("gpsimd", lambda e, du3=du3, B3=B3: e.tensor_tensor(
                         out=h3, in0=du3, in1=B3, op=ALU.mult),
                     reads=[rr("du_w"), rr("B_rep")], writes=[rr("h_buf")])
                yield
                for n in range(8):
                    S.op("vector", lambda e, n=n, dt=dt, n0=n0: e.tensor_tensor_scan(
                             out=h3[:, :, n], data0=dA3[:, :, n], data1=h3[:, :, n],
                             initial=carry[dt][:, n0 + n:n0 + n + 1],
                             op0=ALU.mult, op1=ALU.add),
                         reads=[rr("dA_buf"), rr("h_buf"), rr(f"carry{dt}")],
                         writes=[rr("h_buf")])
                S.op("vector", lambda e, dt=dt, n0=n0: e.tensor_copy(
                         out=carry[dt][:, n0:n0 + 8], in_=h_buf[:, 1016:1024]),
                     reads=[rr("h_buf")], writes=[rr(f"carry{dt}")])
                yield
                C3 = C3f[:, :, n0:n0 + 8]
                S.op("gpsimd", lambda e, C3=C3: e.tensor_tensor(
                         out=dA3, in0=h3, in1=C3, op=ALU.mult),
                     reads=[rr("h_buf"), rr("C_rep")], writes=[rr("dA_buf")])
                yield
                ytgt = (y_h[:, :], rr("y_h")) if hf == 0 else (du_w[:, :], rr("du_w"))
                S.op("vector", lambda e, t=ytgt[0]: e.tensor_reduce(
                         out=t, in_=dA3, axis=AX.X, op=ALU.add),
                     reads=[rr("dA_buf")], writes=[ytgt[1]])
                yield
            S.op("vector", lambda e: e.tensor_tensor(out=y_h[:, :],
                                                     in0=y_h[:, :],
                                                     in1=du_w[:, :],
                                                     op=ALU.add),
                 reads=[rr("y_h"), rr("du_w")], writes=[rr("y_h")])
            S.op("vector", lambda e, dt=dt: e.scalar_tensor_tensor(
                     out=y_h[:, :], in0=uT_w[:, :], scalar=D_sb[dt][:, 0:1],
                     in1=y_h[:, :], op0=ALU.mult, op1=ALU.add),
                 reads=[rr("uT_w"), rr(f"D_sb{dt}"), rr("y_h")], writes=[rr("y_h")])
            r0 = 256 * (w // 16) + 128 * dt
            cc0 = 128 * (w % 16)
            S.dma("qs", lambda e, r0=r0, cc0=cc0: e.dma_start(
                      out=ag_in[r0:r0 + 128, cc0:cc0 + 128],
                      in_=y_h[:, :]),
                  reads=[rr("y_h")], writes=[yw_res[w]])
            yield

    def emit_load_gu(e_, fc, par):
        S.dma("qw", lambda e, e_=e_, fc=fc, par=par: e.dma_start(
                  out=wg_bf[par][:, :], in_=ghi_in[e_, fc, :, :]),
              reads=[rr("ext_in")], writes=[rr(f"wg_bf{par}")])
        S.dma("qw", lambda e, e_=e_, fc=fc, par=par: e.dma_start(
                  out=wu_bf[par][:, :], in_=uhi_in[e_, fc, :, :]),
              reads=[rr("ext_in")], writes=[rr(f"wu_bf{par}")])

    def emit_load_wd(e_):
        for f in range(NFC):
            S.dma("qw", lambda e, e_=e_, f=f: e.dma_start(
                      out=wd_bf[f][:, :], in_=dhi_in[e_, f, :, :]),
                  reads=[rr("ext_in")], writes=[rr(f"wd_bf{f}")])

    def emit_ffn_gu(m, e_, fc, par):
        if e_ < NEXP and fc == 0:
            S.dma("qs", lambda e, e_=e_, m=m: e.dma_start(
                      out=c_rep[:, :],
                      in_=combT_dram[e_:e_ + 1, MTOK * m:MTOK * (m + 1)]
                          .broadcast_to([128, MTOK])),
                  reads=[rr("combT_dram")], writes=[rr("c_rep")])
        gres, ures = rr(f"wg_bf{par}"), rr(f"wu_bf{par}")
        for tc in range(2):
            for k in range(KT):
                S.op("tensor", lambda e, k=k, tc=tc, par=par: e.matmul(
                         pg[:, 512 * tc:512 * (tc + 1)],
                         lhsT=wg_bf[par][:, 128 * k:128 * (k + 1)],
                         rhs=xnT_own[k][:, 512 * tc:512 * (tc + 1)],
                         start=(k == 0), stop=(k == KT - 1)),
                     reads=[gres, rr(f"xnT_own{k}")], writes=[rr("pg")],
                     inc=(k == KT - 1))
            for k in range(KT):
                S.op("tensor", lambda e, k=k, tc=tc, par=par: e.matmul(
                         pu[:, 512 * tc:512 * (tc + 1)],
                         lhsT=wu_bf[par][:, 128 * k:128 * (k + 1)],
                         rhs=xnT_own[k][:, 512 * tc:512 * (tc + 1)],
                         start=(k == 0), stop=(k == KT - 1)),
                     reads=[ures, rr(f"xnT_own{k}")], writes=[rr("pu")],
                     inc=(k == KT - 1))
        S.op("scalar", lambda e: e.activation(out=silu_g[:, :], in_=pg[:, :],
                                              func=AF.Silu),
             reads=[rr("pg")], writes=[rr("silu_g")])
        ares = rr(f"act_f{fc}")
        if e_ < NEXP:
            S.op("vector", lambda e, fc=fc: e.tensor_tensor(
                     out=act_f[fc][:, :], in0=pu[:, :], in1=c_rep[:, :],
                     op=ALU.mult),
                 reads=[rr("pu"), rr("c_rep")], writes=[ares])
            S.op("vector", lambda e, fc=fc: e.tensor_tensor(
                     out=act_f[fc][:, :], in0=act_f[fc][:, :], in1=silu_g[:, :],
                     op=ALU.mult),
                 reads=[ares, rr("silu_g")], writes=[ares])
        else:
            S.op("vector", lambda e, fc=fc: e.tensor_tensor(
                     out=act_f[fc][:, :], in0=pu[:, :], in1=silu_g[:, :],
                     op=ALU.mult),
                 reads=[rr("pu"), rr("silu_g")], writes=[ares])

    def emit_ffn_routed(m, e_):
        prbufs = [(pr[:, 0:512], rr("pr")), (pr[:, 512:1024], rr("pr_hi")),
                  (prx[:, :], rr("prx"))]
        ri = [0]
        for dc in range(KT):
            for tc in range(2):
                pap, pres = prbufs[ri[0] % 3]
                ri[0] += 1
                for f in range(NFC):
                    S.op("tensor", lambda e, dc=dc, tc=tc, f=f, pap=pap: e.matmul(
                             pap,
                             lhsT=wd_bf[f][:, 128 * dc:128 * (dc + 1)],
                             rhs=act_f[f][:, 512 * tc:512 * (tc + 1)],
                             start=(f == 0), stop=(f == NFC - 1)),
                         reads=[rr(f"wd_bf{f}"), rr(f"act_f{f}")], writes=[pres],
                         inc=(f == NFC - 1))
                S.op("vector", lambda e, dc=dc, tc=tc, pap=pap: e.tensor_tensor(
                         out=acc[dc][:, 512 * tc:512 * (tc + 1)], in0=pap,
                         in1=acc[dc][:, 512 * tc:512 * (tc + 1)], op=ALU.add),
                     reads=[pres, rr(f"acc{dc}")], writes=[rr(f"acc{dc}")])

    def emit_macro_switch():
        for dc in range(KT):
            S.dma("qv", lambda e, dc=dc: e.dma_start(
                      out=accA_dram[128 * dc:128 * (dc + 1), :], in_=acc[dc][:, :]),
                  reads=[rr(f"acc{dc}")], writes=[rr("accA")])
            S.op("vector", lambda e, dc=dc: e.memset(acc[dc][:, :], 0.0),
                 reads=[], writes=[rr(f"acc{dc}")])
        for k in range(KT):
            S.dma("qx", lambda e, k=k: e.dma_start(
                      out=xnT_own[k][:, :],
                      in_=xnTB_dram[128 * k:128 * (k + 1), :]),
                  reads=[rr("xnTB")], writes=[rr(f"xnT_own{k}")])

    def emit_allgather():
        from concourse.aot_env import allow_ici_all_to_all

        def fn(e):
            with allow_ici_all_to_all():
                return e.collective_compute(
                    "AllToAll", ALU.bypass, replica_groups=[list(range(NCORE))],
                    ins=[ag_in], outs=[ag_out])
        S.op("gpsimd", fn, reads=yw_res, writes=[rr("ag_out")])

    def emit_assembly(j):
        m = 0 if j < 8 else 1
        jj = j % 8
        for dh in range(2):
            S.dma("qx", lambda e, j=j, dh=dh: e.dma_start(
                      out=x_half[:, :],
                      in_=x_own[128 * j:128 * (j + 1), 1024 * dh:1024 * (dh + 1)]),
                  reads=[rr("ext_in")], writes=[rr("x_half")])
            for dcl in range(8):
                dc = 8 * dh + dcl
                par = dc % 2
                if m == 0:
                    S.dma("qv", lambda e, dc=dc, jj=jj, par=par: e.dma_start(
                              out=acc_in[par],
                              in_=accA_dram[128 * dc:128 * (dc + 1),
                                            128 * jj:128 * (jj + 1)]),
                          reads=[rr("accA")], writes=[rr(f"acc_in{par}")])
                    src_ap = acc_in[par]
                    src_res = rr(f"acc_in{par}")
                else:
                    src_ap = acc[dc][:, 128 * jj:128 * (jj + 1)]
                    src_res = rr(f"acc{dc}")
                pslc = pm[:, 256 + 128 * par:256 + 128 * (par + 1)]
                S.op("tensor", lambda e, src_ap=src_ap, p=pslc: e.transpose(
                         p, in_=src_ap, identity=sid_f32[:, :]),
                     reads=[src_res, rr("sid_f32")], writes=[rr("pm")])
                S.op("vector", lambda e, dcl=dcl, p=pslc: e.scalar_tensor_tensor(
                         out=s_out[:, 128 * dcl:128 * (dcl + 1)], in0=p, scalar=1.0,
                         in1=x_half[:, 128 * dcl:128 * (dcl + 1)],
                         op0=ALU.mult, op1=ALU.add),
                     reads=[rr("pm"), rr("x_half")], writes=[rr("dA_buf")])
                S.dma("qx", lambda e, dc=dc, j=j, par=par: e.dma_start(
                          out=y_in[par],
                          in_=ag_out[128 * dc:128 * (dc + 1),
                                     128 * j:128 * (j + 1)]),
                      reads=[rr("ag_out")], writes=[rr(f"y_in{par}")])
                S.op("tensor", lambda e, par=par, p=pslc: e.transpose(
                         p, in_=y_in[par], identity=sid_f32[:, :]),
                     reads=[rr(f"y_in{par}"), rr("sid_f32")], writes=[rr("pm")])
                S.op("vector", lambda e, dcl=dcl, p=pslc: e.tensor_tensor(
                         out=s_out[:, 128 * dcl:128 * (dcl + 1)], in0=p,
                         in1=s_out[:, 128 * dcl:128 * (dcl + 1)], op=ALU.add),
                     reads=[rr("pm"), rr("dA_buf")], writes=[rr("dA_buf")])
            S.dma("qx", lambda e, j=j, dh=dh: e.dma_start(
                      out=out_own[128 * j:128 * (j + 1), 1024 * dh:1024 * (dh + 1)],
                      in_=s_out[:, :]),
                  reads=[rr("dA_buf")], writes=[rr("out_dram")])

    emit_startup()
    for j in range(OWNT):
        emit_miniprep(j)
    for dc in range(KT):
        S.op("vector", lambda e, dc=dc: e.memset(acc[dc][:, :], 0.0),
             reads=[], writes=[rr(f"acc{dc}")])

    prep_iters = []
    prep_next = [0]
    prep_done = [0]
    scan_iters = []
    scan_next = [0]

    def step_prep(nphase):
        used = 0
        for k in range(nphase):
            if len(prep_iters) < 2 and prep_next[0] < NT and \
                    (not prep_iters or prep_iters[-1][1] >= 2):
                prep_iters.append([emit_prep(prep_next[0]), 0])
                prep_next[0] += 1
            if not prep_iters:
                break
            it = prep_iters[k % len(prep_iters)]
            try:
                next(it[0])
                it[1] += 1
            except StopIteration:
                prep_iters.remove(it)
                prep_done[0] += 1
            used += 1
        return nphase - used

    def step_scan(nphase):
        for k in range(nphase):
            while len(scan_iters) < 1 and scan_next[0] < NT \
                    and scan_next[0] < prep_done[0]:
                scan_iters.append(emit_scan_gen(scan_next[0]))
                scan_next[0] += 1
            if not scan_iters:
                break
            it = scan_iters[k % len(scan_iters)]
            try:
                next(it)
            except StopIteration:
                scan_iters.remove(it)

    a2a_state = [0]
    asm_q = list(range(8))

    def maybe_tail():
        if a2a_state[0] == 0 and scan_next[0] == NT and not scan_iters \
                and not prep_iters and prep_next[0] == NT:
            emit_allgather()
            a2a_state[0] = 1
        elif a2a_state[0] == 1 and asm_q:
            emit_assembly(asm_q.pop(0))

    emit_load_gu(0, 0, 0)
    for m in range(2):
        for e_ in range(NE_ALL):
            emit_load_wd(e_)
            for fc in range(NFC):
                par = (e_ * NFC + fc) % 2
                nxt = (e_ * NFC + fc + 1) % 2
                emit_ffn_gu(m, e_, fc, par)
                if fc < NFC - 1:
                    emit_load_gu(e_, fc + 1, nxt)
                elif e_ < NE_ALL - 1:
                    emit_load_gu(e_ + 1, 0, nxt)
                elif m == 0:
                    emit_load_gu(0, 0, nxt)
                left = step_prep(3)
                step_scan(6 + left)
            emit_ffn_routed(m, e_)
            left = step_prep(3)
            step_scan(14 + left)
            if m == 1:
                maybe_tail()
        if m == 0:
            emit_macro_switch()
    step_prep(100000)
    step_scan(100000)
    if a2a_state[0] == 0:
        emit_allgather()
    for j in asm_q:
        emit_assembly(j)
    for j in range(8, OWNT):
        emit_assembly(j)

    toks = []
    for es in list(S.eng.values()) + list(S.queues.values()):
        for i, sem in enumerate(es.sems):
            if es.counts[i]:
                toks.append((sem, es.counts[i]))
    S.wait_tokens("sync", toks)

    with nc.Block() as block:
        S.emit(block)

    n_inst = sum(len(e.items) for e in S.eng.values())
    stack.close()
    return nc, n_inst


_PROG = {}


def kernel(L0_attn_x, L0_attn_A_log, L0_attn_D, L0_attn_x_proj_w, L0_attn_dt_proj_w,
           L0_attn_dt_proj_b, L0_ffn_gate_w, L0_ffn_wg, L0_ffn_wu, L0_ffn_wd,
           L0_ffn_sg, L0_ffn_su, L0_ffn_sd):
    x = np.ascontiguousarray(np.asarray(L0_attn_x, dtype=np.float32).reshape(TT, D))
    A_log = np.asarray(L0_attn_A_log, dtype=np.float32)
    Dv = np.asarray(L0_attn_D, dtype=np.float32).reshape(D, 1)
    xpw = np.asarray(L0_attn_x_proj_w, dtype=np.float32)
    dtw = np.asarray(L0_attn_dt_proj_w, dtype=np.float32)
    dtb = np.asarray(L0_attn_dt_proj_b, dtype=np.float32)
    gw = np.ascontiguousarray(np.asarray(L0_ffn_gate_w, dtype=np.float32))
    wg_ = np.asarray(L0_ffn_wg, dtype=np.float32)
    wu_ = np.asarray(L0_ffn_wu, dtype=np.float32)
    wd_ = np.asarray(L0_ffn_wd, dtype=np.float32)
    sg_ = np.asarray(L0_ffn_sg, dtype=np.float32)
    su_ = np.asarray(L0_ffn_su, dtype=np.float32)
    sd_ = np.asarray(L0_ffn_sd, dtype=np.float32)

    def gu_planes(w32, sh32):
        allw = np.concatenate(
            [w32, sh32.reshape(D, 2, FEXP).transpose(1, 0, 2)], axis=0)
        t = allw.reshape(NE_ALL, KT, 128, NFC, 128).transpose(0, 3, 2, 1, 4)
        v = np.ascontiguousarray(t).view('<u2').reshape(
            NE_ALL, NFC, 128, KT, 128, 2)
        hi = np.ascontiguousarray(v[..., 1]).reshape(NE_ALL, NFC, 128, D)
        lo = np.ascontiguousarray(v[..., 0]).reshape(NE_ALL, NFC, 128, D)
        return hi.view(ml_dtypes.bfloat16), lo.view(ml_dtypes.bfloat16)

    def d_planes(w32, sh32):
        allw = np.concatenate([w32, sh32.reshape(2, FEXP, D)], axis=0)
        t = allw.reshape(NE_ALL, NFC, 128, D)
        v = np.ascontiguousarray(t).view('<u2').reshape(NE_ALL, NFC, 128, D, 2)
        hi = np.ascontiguousarray(v[..., 1])
        lo = np.ascontiguousarray(v[..., 0])
        return hi.view(ml_dtypes.bfloat16), lo.view(ml_dtypes.bfloat16)

    ghi_h, glo_h = gu_planes(wg_, sg_)
    uhi_h, ulo_h = gu_planes(wu_, su_)
    dhi_h, dlo_h = d_planes(wd_, sd_)

    xpwT_h = np.ascontiguousarray(xpw.T)
    id_f32_h = np.eye(128, dtype=np.float32)

    if "nc" not in _PROG:
        _PROG["nc"], _PROG["n"] = build_program()
    nc = _PROG["nc"]

    in_maps = []
    for c in range(NCORE):
        dsl = slice(DSL * c, DSL * (c + 1))
        in_maps.append(dict(
            x_all=x,
            x_sl=np.ascontiguousarray(x[:, dsl]),
            x_own=np.ascontiguousarray(x[2048 * c:2048 * (c + 1), :]),
            alog_sl=np.ascontiguousarray(A_log[dsl, :]),
            d_sl=np.ascontiguousarray(Dv[dsl, :]),
            xpwT=xpwT_h,
            dtwT_sl=np.ascontiguousarray(dtw[dsl, :].T),
            bias_sl=np.ascontiguousarray(dtb[dsl].reshape(DSL, 1)),
            gate_w=gw,
            ghi=ghi_h, uhi=uhi_h, dhi=dhi_h,
            glo=glo_h, ulo=ulo_h, dlo=dlo_h,
            id_f32=id_f32_h,
        ))

    res = run_bass_kernel_spmd(nc, in_maps, core_ids=list(range(NCORE)))
    out = np.concatenate([res.results[c]["out_own"] for c in range(NCORE)], axis=0)
    return out.reshape(2, 8192, D)


if __name__ == "__main__":
    nc_, n_ = build_program()
    print("built", n_, "instructions")


# version 15
# speedup vs baseline: 1.1224x; 1.1224x over previous
import os
os.environ.setdefault("TRN_ALLOW_ICI_ALL_TO_ALL", "1")
import numpy as np
import ml_dtypes
from contextlib import ExitStack

import concourse.bass as bass
import concourse.mybir as mybir
from concourse.bass_utils import run_bass_kernel_spmd

F32 = mybir.dt.float32
BF16 = mybir.dt.bfloat16
U32 = mybir.dt.uint32
AF = mybir.ActivationFunctionType
ALU = mybir.AluOpType
AX = mybir.AxisListType

NCORE = 8
TT = 16384
D = 2048
DST = 16
DTR = 128
XDB = 160
NEXP = 32
FEXP = 768
DSL = 256
NT = 128
OWNT = 16
KT = 16
MTOK = 1024
NFC = 6
NE_ALL = 34



class Res:
    __slots__ = ("name", "lw", "readers")

    def __init__(self, name):
        self.name = name
        self.lw = {}
        self.readers = {}


class EngS:
    __slots__ = ("name", "sems", "counts", "idx", "waited", "items", "host",
                 "pend_reads", "pend_writes")

    def __init__(self, name, sems):
        self.name = name
        self.sems = sems
        self.counts = [0] * len(sems)
        self.idx = 0
        self.waited = {}
        self.items = []
        self.host = None
        self.pend_reads = []
        self.pend_writes = []


def _merge(d, sem, val):
    if d.get(sem, 0) < val:
        d[sem] = val


class Sched:
    def __init__(self, nc, stack):
        self.nc = nc
        self.stack = stack
        self.eng = {}
        self.queues = {}

    def add_engine(self, name, nsem):
        sems = [self.stack.enter_context(self.nc.semaphore(f"es_{name}{i}"))
                for i in range(nsem)]
        self.eng[name] = EngS(name, sems)

    def add_queue(self, name, ename, nsem):
        sems = [self.stack.enter_context(self.nc.semaphore(f"qs_{name}{i}"))
                for i in range(nsem)]
        q = EngS(name, sems)
        q.host = ename
        self.queues[name] = q

    def _waits(self, e, reads, writes):
        need = {}
        for r in reads:
            for sem, val in r.lw.items():
                _merge(need, sem, val)
        for w in writes:
            for sem, val in w.lw.items():
                _merge(need, sem, val)
            for sem, val in w.readers.items():
                _merge(need, sem, val)
        out = []
        for sem, val in need.items():
            if e.waited.get(sem, 0) < val:
                out.append((sem, val))
            _merge(e.waited, sem, val)
        return out

    def _finish(self, e, tok, reads, writes):
        if tok is None:
            e.pend_reads.extend(reads)
            e.pend_writes.extend(writes)
            return None
        sem, val = tok
        for r in e.pend_reads:
            _merge(r.readers, sem, val)
        for w in e.pend_writes:
            _merge(w.lw, sem, val)
            w.readers = {}
        e.pend_reads = []
        e.pend_writes = []
        for r in reads:
            _merge(r.readers, sem, val)
        for w in writes:
            _merge(w.lw, sem, val)
            w.readers = {}
        return tok

    def op(self, ename, fn, reads=(), writes=(), inc=True):
        e = self.eng[ename]
        waits = self._waits(e, reads, writes)
        tok = None
        if inc:
            i = e.idx
            e.idx = (e.idx + 1) % len(e.sems)
            e.counts[i] += 1
            tok = (e.sems[i], e.counts[i])
        e.items.append((waits, fn, tok, 1))
        return self._finish(e, tok, reads, writes)

    def dma(self, qname, fn, reads=(), writes=()):
        q = self.queues[qname]
        e = self.eng[q.host]
        waits = self._waits(e, reads, writes)
        i = q.idx
        q.idx = (q.idx + 1) % len(q.sems)
        q.counts[i] += 16
        tok = (q.sems[i], q.counts[i])
        e.items.append((waits, fn, tok, 16))
        return self._finish(e, tok, reads, writes)

    def wait_tokens(self, ename, toks):
        e = self.eng[ename]
        need = {}
        for sem, val in toks:
            _merge(need, sem, val)
        waits = []
        for sem, val in need.items():
            if e.waited.get(sem, 0) < val:
                waits.append((sem, val))
            _merge(e.waited, sem, val)
        if waits:
            e.items.append((waits, None, None, 0))

    def emit(self, block):
        def mk(e):
            def run(eng):
                for waits, fn, tok, amt in e.items:
                    for sem, val in waits:
                        eng.wait_ge(sem, val)
                    if fn is not None:
                        inst = fn(eng)
                        if tok is not None:
                            inst.then_inc(tok[0], amt)
            return run
        block.sync(mk(self.eng["sync"]))
        block.vector(mk(self.eng["vector"]))
        block.scalar(mk(self.eng["scalar"]))
        block.tensor(mk(self.eng["tensor"]))
        block.gpsimd(mk(self.eng["gpsimd"]))



def build_program():
    nc = bass.Bass("TRN2", target_bir_lowering=False, debug=False)

    def din(name, shape, dtype=F32):
        return nc.dram_tensor(name, list(shape), dtype, kind="ExternalInput").ap()

    x_all = din("x_all", [TT, D])
    x_sl = din("x_sl", [TT, DSL])
    x_own = din("x_own", [2048, D])
    alog_sl = din("alog_sl", [DSL, DST])
    d_sl = din("d_sl", [DSL, 1])
    xpwT_in = din("xpwT", [D, XDB])
    dtwT_in = din("dtwT_sl", [DTR, DSL])
    bias_in = din("bias_sl", [DSL, 1])
    gate_in = din("gate_w", [D, NEXP])
    ghi_in = din("ghi", [NE_ALL, NFC, 128, D], BF16)
    uhi_in = din("uhi", [NE_ALL, NFC, 128, D], BF16)
    dhi_in = din("dhi", [NE_ALL, NFC, 128, D], BF16)
    glo_in = din("glo", [NE_ALL, NFC, 128, D], BF16)
    ulo_in = din("ulo", [NE_ALL, NFC, 128, D], BF16)
    dlo_in = din("dlo", [NE_ALL, NFC, 128, D], BF16)
    idf_in = din("id_f32", [128, 128], F32)

    out_own = nc.dram_tensor("out_own", [2048, D], F32, kind="ExternalOutput").ap()

    dT_dram = nc.dram_tensor("dT_dram", [DSL, TT], F32, kind="Internal").ap()
    uT_dram = nc.dram_tensor("uT_dram", [DSL, TT], F32, kind="Internal").ap()
    Bflat = nc.dram_tensor("Bflat", [TT, DST], F32, kind="Internal").ap()
    Cflat = nc.dram_tensor("Cflat", [TT, DST], F32, kind="Internal").ap()
    xnTB_dram = nc.dram_tensor("xnTB_dram", [D, MTOK], BF16, kind="Internal").ap()
    accA_dram = nc.dram_tensor("accA_dram", [D, MTOK], F32, kind="Internal").ap()
    combT_dram = nc.dram_tensor("combT_dram", [NEXP, 2048], F32, kind="Internal").ap()
    ag_in = nc.dram_tensor("ag_in", [D, 2048], F32, kind="Internal").ap()
    ag_out = nc.dram_tensor("ag_out", [D, 2048], F32, kind="Internal").ap()

    stack = ExitStack()
    E = stack.enter_context

    sbufs = {}

    def sbuf(name, shape, dtype):
        t = E(nc.sbuf_tensor(name, list(shape), dtype))
        sbufs[name] = t
        return t

    xnT_own = [sbuf(f"xnT_own{k}", [128, MTOK], BF16) for k in range(KT)]
    acc = [sbuf(f"acc{j}", [128, MTOK], F32) for j in range(KT)]
    cmb = sbuf("cmb", [128, OWNT * NEXP], F32)
    c_rep = sbuf("c_rep", [128, MTOK], F32)
    wg_bf = [sbuf(f"wg_bf{p}", [128, D], BF16) for p in range(2)]
    wu_bf = [sbuf(f"wu_bf{p}", [128, D], BF16) for p in range(2)]
    wd_bf = [sbuf(f"wd_bf{f}", [128, D], BF16) for f in range(NFC)]
    x_half = sbuf("x_half", [128, 1024], F32)
    xsl_tile = sbuf("xsl_tile", [128, DSL], F32)
    xn_bf = sbuf("xn_bf", [128, D], BF16)
    xnT_stage = sbuf("xnT_stage", [128, D], BF16)
    xbt = sbuf("xbt", [128, 256], BF16)
    xnT_f32s = sbuf("xnT_f32s", [128, 128], F32)
    dpre_sb = sbuf("dpre_sb", [128, DSL], F32)
    sp_t1 = sbuf("sp_t1", [128, DSL], F32)
    xn_sl = [sbuf(f"xn_sl{p}", [128, DSL], F32) for p in range(2)]
    tpose_f32 = sbuf("tpose_f32", [128, 128], F32)
    stat = sbuf("stat", [128, 16], F32)
    xdbl_s = sbuf("xdbl_s", [128, XDB], F32)
    sc_sb = sbuf("sc_sb", [128, NEXP], F32)
    m8_sb = sbuf("m8_sb", [128, 8], F32)
    msk_sb = sbuf("msk_sb", [128, NEXP], F32)
    den_sb = sbuf("den_sb", [128, 2], F32)
    dT_w = sbuf("dT_w", [128, 128], F32)
    uT_w = sbuf("uT_w", [128, 128], F32)
    du_w = sbuf("du_w", [128, 128], F32)
    B_rep = sbuf("B_rep", [128, 128 * DST], F32)
    C_rep = sbuf("C_rep", [128, 128 * DST], F32)
    dA_buf = sbuf("dA_buf", [128, 1024], F32)
    h_buf = sbuf("h_buf", [128, 1024], F32)
    carry = [sbuf(f"carry{t}", [128, DST], F32) for t in range(2)]
    y_h = sbuf("y_h", [128, 128], F32)
    A_sb = [sbuf(f"A_sb{t}", [128, DST], F32) for t in range(2)]
    D_sb = [sbuf(f"D_sb{t}", [128, 1], F32) for t in range(2)]
    silu_g = sbuf("silu_g", [128, MTOK], BF16)
    act_f = [sbuf(f"act_f{f}", [128, MTOK], BF16) for f in range(NFC)]
    sid_f32 = sbuf("sid_f32", [128, 128], F32)
    xpwT_bf = sbuf("xpwT_bf", [128, KT * XDB], BF16)
    dtwT_bf = sbuf("dtwT_bf", [128, DSL], BF16)
    bias2 = sbuf("bias2", [128, 6], F32)
    bias_sb = [bias2[:, 0:1], bias2[:, 1:2]]
    sconst = bias2[:, 2:6]

    s_out = dA_buf[:, :]
    gate_sb = c_rep[:, 0:512]
    xn_f32q = dA_buf[:, 0:512]
    stg_small = h_buf
    alog_sb = h_buf[:, 512:544]
    y_in = [cmb[:, 0:128], cmb[:, 128:256]]
    acc_in = [cmb[:, 256:384], cmb[:, 384:512]]
    ctT_stage = tpose_f32[0:NEXP, :]
    dr_bf = xbt[:, 0:128]
    drT_bf = xbt[:, 128:256]

    pg = E(nc.psum_tensor("pg", [128, MTOK], F32))
    pu = E(nc.psum_tensor("pu", [128, MTOK], F32))
    pr = E(nc.psum_tensor("pr", [128, MTOK], F32))
    pm = E(nc.psum_tensor("pm", [128, 512], F32))
    prx = E(nc.psum_tensor("prx", [128, 512], F32))

    S = Sched(nc, stack)
    for en, ns in (("sync", 2), ("vector", 2), ("scalar", 2), ("tensor", 2),
                   ("gpsimd", 2)):
        S.add_engine(en, ns)
    S.add_queue("qx", "sync", 4)
    S.add_queue("qw", "sync", 8)
    S.add_queue("qp", "scalar", 4)
    S.add_queue("qs", "scalar", 4)
    S.add_queue("qt", "sync", 4)
    S.add_queue("qv", "scalar", 4)

    R = {name: Res(name) for name in sbufs}
    for extra in ("pg", "pu", "pr", "pr_hi", "prx", "pm", "xnTB", "accA", "ag_out",
                  "combT_dram", "ext_in", "out_dram",
                  "y_in0", "y_in1", "acc_in0", "acc_in1"):
        R[extra] = Res(extra)
    prep_res = [Res(f"prep{w}") for w in range(NT)]
    yw_res = [Res(f"yw{w}") for w in range(NT)]
    rr = R.__getitem__

    def cast_bf16(engine, dst_ap, src_ap, reads, writes):
        if engine == "vector":
            S.op("vector", lambda e, d=dst_ap, s=src_ap: e.tensor_copy(out=d, in_=s),
                 reads=reads, writes=writes)
        else:
            S.op("scalar", lambda e, d=dst_ap, s=src_ap: e.copy(out=d, in_=s),
                 reads=reads, writes=writes)

    def emit_startup():
        for fn, w in (
            ((lambda e: e.dma_start(out=sid_f32[:, :], in_=idf_in)), "sid_f32"),
            ((lambda e: e.dma_start(out=alog_sb[:, 0:DST], in_=alog_sl[0:128, :])), "h_buf"),
            ((lambda e: e.dma_start(out=alog_sb[:, DST:2 * DST], in_=alog_sl[128:256, :])), "h_buf"),
            ((lambda e: e.dma_start(out=D_sb[0][:, :], in_=d_sl[0:128, :])), "D_sb0"),
            ((lambda e: e.dma_start(out=D_sb[1][:, :], in_=d_sl[128:256, :])), "D_sb1"),
            ((lambda e: e.dma_start(out=bias_sb[0], in_=bias_in[0:128, :])), "bias2"),
            ((lambda e: e.dma_start(out=bias_sb[1], in_=bias_in[128:256, :])), "bias2"),
        ):
            S.dma("qx", fn, reads=[rr("ext_in")], writes=[rr(w)])
        for ci, cv in ((0, 1e-5), (1, -0.5), (2, -1.0), (3, 1.0)):
            S.op("vector", lambda e, ci=ci, cv=cv: e.memset(sconst[:, ci:ci + 1], cv),
                 reads=[], writes=[rr("bias2")])
        gk = gate_in.rearrange("(k p) e -> k p e", p=128)
        S.dma("qx", lambda e: e.dma_start(
                  out=gate_sb.rearrange("p (k e) -> p k e", k=KT),
                  in_=gk.transpose([1, 0, 2])),
              reads=[rr("ext_in")], writes=[rr("c_rep")])
        S.op("scalar", lambda e: e.activation(out=alog_sb[:, :], in_=alog_sb[:, :],
                                              func=AF.Exp),
             reads=[rr("h_buf")], writes=[rr("h_buf")])
        for t in range(2):
            S.op("vector", lambda e, t=t: e.tensor_scalar_mul(
                     out=A_sb[t][:, :], in0=alog_sb[:, DST * t:DST * (t + 1)],
                     scalar1=-1.0),
                 reads=[rr("h_buf")], writes=[rr(f"A_sb{t}")])
        xk = xpwT_in.rearrange("(k p) e -> k p e", p=128)
        for k in range(KT):
            S.dma("qx", lambda e, k=k: e.dma_start(out=stg_small[:, 0:XDB],
                                                   in_=xk[k, :, :]),
                  reads=[rr("ext_in")], writes=[rr("h_buf")])
            cast_bf16("vector", xpwT_bf[:, XDB * k:XDB * (k + 1)],
                      stg_small[:, 0:XDB], [rr("h_buf")], [rr("xpwT_bf")])
        S.dma("qx", lambda e: e.dma_start(out=stg_small[:, 0:DSL], in_=dtwT_in),
              reads=[rr("ext_in")], writes=[rr("h_buf")])
        cast_bf16("vector", dtwT_bf[:, :], stg_small[:, 0:DSL], [rr("h_buf")],
                  [rr("dtwT_bf")])

    def emit_stats_xn(src_rows_fn, f32_halves, so=0):
        st = stat[:, :]
        for h in range(2):
            S.dma("qx", lambda e, h=h: e.dma_start(out=x_half[:, :],
                                                   in_=src_rows_fn(h)),
                  reads=[rr("ext_in")], writes=[rr("x_half")])
            S.op("vector", lambda e, h=h: e.tensor_reduce(
                     out=st[:, (so if h == 0 else so + 2):(so + 1 if h == 0 else so + 3)],
                     in_=x_half[:, :], axis=AX.X, op=ALU.add),
                 reads=[rr("x_half")], writes=[rr("stat")])
            S.op("vector", lambda e, h=h: e.scalar_tensor_tensor(
                     out=xn_bf[:, 1024 * h:1024 * (h + 1)], in0=x_half[:, :],
                     scalar=1.0, in1=x_half[:, :], op0=ALU.mult, op1=ALU.mult,
                     accum_out=st[:, (so + 1 if h == 0 else so + 3):(so + 2 if h == 0 else so + 4)]),
                 reads=[rr("x_half")], writes=[rr("xn_bf"), rr("stat")])
            if h == 0:
                continue
            S.op("vector", lambda e: e.tensor_tensor(out=st[:, so + 0:so + 1], in0=st[:, so + 0:so + 1],
                                                     in1=st[:, so + 2:so + 3], op=ALU.add),
                 reads=[rr("stat")], writes=[rr("stat")])
            S.op("vector", lambda e: e.tensor_tensor(out=st[:, so + 1:so + 2], in0=st[:, so + 1:so + 2],
                                                     in1=st[:, so + 3:so + 4], op=ALU.add),
                 reads=[rr("stat")], writes=[rr("stat")])
            S.op("vector", lambda e: e.tensor_scalar_mul(out=st[:, so + 2:so + 3],
                                                         in0=st[:, so + 0:so + 1],
                                                         scalar1=1.0 / D),
                 reads=[rr("stat")], writes=[rr("stat")])
            S.op("vector", lambda e: e.tensor_tensor(out=st[:, so + 3:so + 4], in0=st[:, so + 2:so + 3],
                                                     in1=st[:, so + 2:so + 3], op=ALU.mult),
                 reads=[rr("stat")], writes=[rr("stat")])
            S.op("vector", lambda e: e.scalar_tensor_tensor(
                     out=st[:, so + 4:so + 5], in0=st[:, so + 1:so + 2], scalar=1.0 / D, in1=st[:, so + 3:so + 4],
                     op0=ALU.mult, op1=ALU.subtract),
                 reads=[rr("stat")], writes=[rr("stat")])
            S.op("scalar", lambda e: e.activation(out=st[:, so + 4:so + 5], in_=st[:, so + 4:so + 5],
                                                  func=AF.Ln, bias=sconst[:, 0:1]),
                 reads=[rr("stat"), rr("bias2")], writes=[rr("stat")])
            S.op("scalar", lambda e: e.activation(out=st[:, so + 5:so + 6], in_=st[:, so + 4:so + 5],
                                                  func=AF.Exp, scale=sconst[:, 1:2]),
                 reads=[rr("stat"), rr("bias2")], writes=[rr("stat")])
            S.op("vector", lambda e: e.scalar_tensor_tensor(
                     out=st[:, so + 6:so + 7], in0=st[:, so + 2:so + 3], scalar=-1.0, in1=st[:, so + 5:so + 6],
                     op0=ALU.mult, op1=ALU.mult),
                 reads=[rr("stat")], writes=[rr("stat")])
        for h in range(2):
            S.dma("qx", lambda e, h=h: e.dma_start(out=x_half[:, :],
                                                   in_=src_rows_fn(h)),
                  reads=[rr("ext_in")], writes=[rr("x_half")])
            S.op("scalar", lambda e, h=h: e.activation(
                     out=xn_bf[:, 1024 * h:1024 * (h + 1)], in_=x_half[:, :],
                     func=AF.Identity, scale=st[:, so + 5:so + 6], bias=st[:, so + 6:so + 7]),
                 reads=[rr("x_half"), rr("stat")], writes=[rr("xn_bf")])
            if f32_halves:
                for q2 in range(2):
                    S.op("scalar", lambda e, q2=q2: e.activation(
                             out=xn_f32q, in_=x_half[:, 512 * q2:512 * (q2 + 1)],
                             func=AF.Identity, scale=st[:, so + 5:so + 6], bias=st[:, so + 6:so + 7]),
                         reads=[rr("x_half"), rr("stat")], writes=[rr("dA_buf")])
                    yield 2 * h + q2

    def xnT_k(k, dst_ap, dst_res):
        S.dma("qt", lambda e, k=k, ap=dst_ap: e.dma_start_transpose(
                  out=ap, in_=xn_bf[:, 128 * k:128 * (k + 1)]),
              reads=[rr("xn_bf")], writes=[dst_res])

    def emit_miniprep(j):
        src = lambda h: x_own[128 * j:128 * (j + 1), 1024 * h:1024 * (h + 1)]
        for q in emit_stats_xn(src, True):
            for kk in range(4):
                k = 4 * q + kk
                reg = kk % 2
                pslc = pm[:, 256 + 128 * reg:256 + 128 * (reg + 1)]
                S.op("tensor", lambda e, kk=kk, p=pslc: e.transpose(
                         p, in_=xn_f32q[:, 128 * kk:128 * (kk + 1)],
                         identity=sid_f32[:, :]),
                     reads=[rr("dA_buf"), rr("sid_f32")], writes=[rr("pm")])
                S.op("vector", lambda e, p=pslc: e.tensor_copy(
                         out=xnT_f32s[:, :], in_=p),
                     reads=[rr("pm")], writes=[rr("xnT_f32s")])
                S.op("tensor", lambda e, k=k: e.matmul(
                         pr[:, 0:NEXP], lhsT=xnT_f32s[:, :],
                         rhs=gate_sb[:, NEXP * k:NEXP * (k + 1)],
                         start=(k == 0), stop=(k == KT - 1)),
                     reads=[rr("xnT_f32s"), rr("c_rep")], writes=[rr("pr")],
                     inc=(k == KT - 1))
        if j < 8:
            for k in range(KT):
                xnT_k(k, xnT_own[k][:, 128 * j:128 * (j + 1)], rr(f"xnT_own{k}"))
        else:
            jj = j - 8
            for k in range(KT):
                par = k % 2
                xnT_k(k, xbt[:, 128 * par:128 * (par + 1)], rr("xbt"))
                S.dma("qx", lambda e, k=k, jj=jj, par=par: e.dma_start(
                          out=xnTB_dram[128 * k:128 * (k + 1),
                                        128 * jj:128 * (jj + 1)],
                          in_=xbt[:, 128 * par:128 * (par + 1)]),
                      reads=[rr("xbt")], writes=[rr("xnTB")])
        S.op("vector", lambda e: e.tensor_copy(out=sc_sb[:, :], in_=pr[:, 0:NEXP]),
             reads=[rr("pr")], writes=[rr("sc_sb")])
        S.op("scalar", lambda e: e.activation(out=sc_sb[:, :], in_=sc_sb[:, :],
                                              func=AF.Exp, scale=sconst[:, 2:3]),
             reads=[rr("sc_sb"), rr("bias2")], writes=[rr("sc_sb")])
        S.op("vector", lambda e: e.tensor_scalar_add(out=sc_sb[:, :],
                                                     in0=sc_sb[:, :], scalar1=1.0),
             reads=[rr("sc_sb")], writes=[rr("sc_sb")])
        S.op("vector", lambda e: e.reciprocal(out=sc_sb[:, :], in_=sc_sb[:, :]),
             reads=[rr("sc_sb")], writes=[rr("sc_sb")])
        S.op("vector", lambda e: e.max(out=m8_sb[:, :], in_=sc_sb[:, :]),
             reads=[rr("sc_sb")], writes=[rr("m8_sb")])
        S.op("vector", lambda e: e.tensor_single_scalar(
                 out=msk_sb[:, :], in_=sc_sb[:, :], scalar=m8_sb[:, 5:6],
                 op=ALU.is_ge),
             reads=[rr("sc_sb"), rr("m8_sb")], writes=[rr("msk_sb")])
        S.op("vector", lambda e: e.tensor_reduce(out=den_sb[:, 0:1],
                                                 in_=m8_sb[:, 0:6], axis=AX.X,
                                                 op=ALU.add),
             reads=[rr("m8_sb")], writes=[rr("den_sb")])
        S.op("vector", lambda e: e.tensor_scalar_add(out=den_sb[:, 0:1],
                                                     in0=den_sb[:, 0:1],
                                                     scalar1=1e-20),
             reads=[rr("den_sb")], writes=[rr("den_sb")])
        S.op("vector", lambda e: e.reciprocal(out=den_sb[:, 1:2],
                                              in_=den_sb[:, 0:1]),
             reads=[rr("den_sb")], writes=[rr("den_sb")])
        S.op("vector", lambda e: e.tensor_tensor(out=msk_sb[:, :], in0=msk_sb[:, :],
                                                 in1=sc_sb[:, :], op=ALU.mult),
             reads=[rr("msk_sb"), rr("sc_sb")], writes=[rr("msk_sb")])
        cslc = cmb[:, NEXP * j:NEXP * (j + 1)]
        S.op("vector", lambda e, c=cslc: e.tensor_scalar_mul(
                 out=c, in0=msk_sb[:, :], scalar1=den_sb[:, 1:2]),
             reads=[rr("msk_sb"), rr("den_sb")], writes=[rr("cmb")])
        S.op("tensor", lambda e, c=cslc: e.transpose(
                 pm[0:NEXP, 256:384], in_=c, identity=sid_f32[:, :]),
             reads=[rr("cmb"), rr("sid_f32")], writes=[rr("pm")])
        S.op("vector", lambda e: e.tensor_copy(out=ctT_stage[:, :],
                                               in_=pm[0:NEXP, 256:384]),
             reads=[rr("pm")], writes=[rr("tpose_f32")])
        S.dma("qp", lambda e, j=j: e.dma_start(
                  out=combT_dram[:, 128 * j:128 * (j + 1)], in_=ctT_stage[:, :]),
              reads=[rr("tpose_f32")], writes=[rr("combT_dram")])

    def emit_prep(i):
        so = 8 * (i % 2)
        xp = i % 2
        src = lambda h: x_all[128 * i:128 * (i + 1), 1024 * h:1024 * (h + 1)]
        for _ in emit_stats_xn(src, False, so):
            pass
        S.dma("qx", lambda e, i=i: e.dma_start(
                  out=xsl_tile[:, :], in_=x_sl[128 * i:128 * (i + 1), :]),
              reads=[rr("ext_in")], writes=[rr("xsl_tile")])
        S.op("scalar", lambda e, so=so, xp=xp: e.activation(
                 out=xn_sl[xp][:, :], in_=xsl_tile[:, :],
                 func=AF.Identity, scale=stat[:, so + 5:so + 6],
                 bias=stat[:, so + 6:so + 7]),
             reads=[rr("xsl_tile"), rr("stat")], writes=[rr(f"xn_sl{xp}")])
        for k in range(KT):
            xnT_k(k, xnT_stage[:, 128 * k:128 * (k + 1)], rr("xnT_stage"))
        yield
        yield
        for k in range(KT):
            S.op("tensor", lambda e, k=k: e.matmul(
                     pm[:, 0:XDB], lhsT=xnT_stage[:, 128 * k:128 * (k + 1)],
                     rhs=xpwT_bf[:, XDB * k:XDB * (k + 1)],
                     start=(k == 0), stop=(k == KT - 1)),
                 reads=[rr("xnT_stage"), rr("xpwT_bf")], writes=[rr("pm")],
                 inc=(k == KT - 1))
        S.op("scalar", lambda e: e.copy(out=xdbl_s[:, 128:160], in_=pm[:, 128:160]),
             reads=[rr("pm")], writes=[rr("xdbl_s")])
        S.dma("qp", lambda e, i=i: e.dma_start(out=Bflat[128 * i:128 * (i + 1), :],
                                               in_=xdbl_s[:, 128:144]),
              reads=[rr("xdbl_s")], writes=[prep_res[i]])
        S.dma("qp", lambda e, i=i: e.dma_start(out=Cflat[128 * i:128 * (i + 1), :],
                                               in_=xdbl_s[:, 144:160]),
              reads=[rr("xdbl_s")], writes=[prep_res[i]])
        S.op("scalar", lambda e: e.copy(out=dr_bf, in_=pm[:, 0:DTR]),
             reads=[rr("pm")], writes=[rr("xbt")])
        S.dma("qt", lambda e: e.dma_start_transpose(out=drT_bf, in_=dr_bf),
              reads=[rr("xbt")], writes=[rr("xbt")])
        yield
        for dt in range(2):
            S.op("tensor", lambda e, dt=dt: e.matmul(
                     pm[:, 128 * dt:128 * (dt + 1)],
                     lhsT=dtwT_bf[:, 128 * dt:128 * (dt + 1)],
                     rhs=drT_bf, start=True, stop=True),
                 reads=[rr("xbt"), rr("dtwT_bf")], writes=[rr("pm")])
        for dt in range(2):
            S.op("scalar", lambda e, dt=dt: e.activation(
                     out=sp_t1[:, 128 * dt:128 * (dt + 1)],
                     in_=pm[:, 128 * dt:128 * (dt + 1)],
                     func=AF.Abs, bias=bias_sb[dt]),
                 reads=[rr("pm"), rr("bias2")], writes=[rr("sp_t1")])
            S.op("scalar", lambda e, dt=dt: e.activation(
                     out=sp_t1[:, 128 * dt:128 * (dt + 1)],
                     in_=sp_t1[:, 128 * dt:128 * (dt + 1)],
                     func=AF.Exp, scale=sconst[:, 2:3]),
                 reads=[rr("sp_t1"), rr("bias2")], writes=[rr("sp_t1")])
            S.op("scalar", lambda e, dt=dt: e.activation(
                     out=sp_t1[:, 128 * dt:128 * (dt + 1)],
                     in_=sp_t1[:, 128 * dt:128 * (dt + 1)],
                     func=AF.Ln, bias=sconst[:, 3:4]),
                 reads=[rr("sp_t1"), rr("bias2")], writes=[rr("sp_t1")])
            S.op("scalar", lambda e, dt=dt: e.activation(
                     out=dpre_sb[:, 128 * dt:128 * (dt + 1)],
                     in_=pm[:, 128 * dt:128 * (dt + 1)],
                     func=AF.Relu, bias=bias_sb[dt]),
                 reads=[rr("pm"), rr("bias2")], writes=[rr("dpre_sb")])
        yield
        for dt in range(2):
            tb = tpose_f32 if dt == 0 else xnT_f32s
            tbr = rr("tpose_f32") if dt == 0 else rr("xnT_f32s")
            S.op("gpsimd", lambda e, dt=dt, tb=tb: e.tensor_tensor(
                     out=tb[:, :], in0=sp_t1[:, 128 * dt:128 * (dt + 1)],
                     in1=dpre_sb[:, 128 * dt:128 * (dt + 1)], op=ALU.add),
                 reads=[rr("sp_t1"), rr("dpre_sb")], writes=[tbr])
            S.dma("qp", lambda e, dt=dt, i=i, tb=tb: e.dma_start(
                      out=dT_dram[128 * dt:128 * (dt + 1), 128 * i:128 * (i + 1)],
                      in_=tb[:, :]),
                  reads=[tbr], writes=[prep_res[i]])
        yield
        for dt in range(2):
            pslc = pm[:, 256 + 128 * dt:256 + 128 * (dt + 1)]
            tb = tpose_f32 if dt == 0 else xnT_f32s
            tbr = rr("tpose_f32") if dt == 0 else rr("xnT_f32s")
            S.op("tensor", lambda e, dt=dt, p=pslc, xp=xp: e.transpose(
                     p, in_=xn_sl[xp][:, 128 * dt:128 * (dt + 1)],
                     identity=sid_f32[:, :]),
                 reads=[rr(f"xn_sl{xp}"), rr("sid_f32")], writes=[rr("pm")])
            S.op("scalar", lambda e, p=pslc, tb=tb: e.copy(out=tb[:, :], in_=p),
                 reads=[rr("pm")], writes=[tbr])
            S.dma("qp", lambda e, dt=dt, i=i, tb=tb: e.dma_start(
                      out=uT_dram[128 * dt:128 * (dt + 1), 128 * i:128 * (i + 1)],
                      in_=tb[:, :]),
                  reads=[tbr], writes=[prep_res[i]])

    def emit_scan_gen(w):
        S.dma("qs", lambda e, w=w: e.dma_start(
                  out=B_rep[:, :], in_=Bflat[128 * w:128 * (w + 1), :]
                      .rearrange("l n -> (l n)").unsqueeze(0)
                      .broadcast_to([128, 128 * DST])),
              reads=[prep_res[w]], writes=[rr("B_rep")])
        S.dma("qs", lambda e, w=w: e.dma_start(
                  out=C_rep[:, :], in_=Cflat[128 * w:128 * (w + 1), :]
                      .rearrange("l n -> (l n)").unsqueeze(0)
                      .broadcast_to([128, 128 * DST])),
              reads=[prep_res[w]], writes=[rr("C_rep")])
        B3f = B_rep[:, :].rearrange("p (l n) -> p l n", n=DST)
        C3f = C_rep[:, :].rearrange("p (l n) -> p l n", n=DST)
        dA3 = dA_buf[:, :].rearrange("p (l n) -> p l n", n=8)
        h3 = h_buf[:, :].rearrange("p (l n) -> p l n", n=8)
        yield
        for dt in range(2):
            S.dma("qs", lambda e, dt=dt, w=w: e.dma_start(
                      out=dT_w[:, :],
                      in_=dT_dram[128 * dt:128 * (dt + 1), 128 * w:128 * (w + 1)]),
                  reads=[prep_res[w]], writes=[rr("dT_w")])
            S.dma("qs", lambda e, dt=dt, w=w: e.dma_start(
                      out=uT_w[:, :],
                      in_=uT_dram[128 * dt:128 * (dt + 1), 128 * w:128 * (w + 1)]),
                  reads=[prep_res[w]], writes=[rr("uT_w")])
            yield
            S.op("vector", lambda e: e.tensor_tensor(
                     out=du_w[:, :], in0=dT_w[:, :], in1=uT_w[:, :], op=ALU.mult),
                 reads=[rr("dT_w"), rr("uT_w")], writes=[rr("du_w")])
            if w % 64 == 0:
                S.op("vector", lambda e, dt=dt: e.memset(carry[dt][:, :], 0.0),
                     reads=[], writes=[rr(f"carry{dt}")])
            for hf in range(2):
                n0 = 8 * hf
                dT3 = dT_w[:, :].unsqueeze(2).broadcast_to([128, 128, 8])
                A3 = A_sb[dt][:, n0:n0 + 8].unsqueeze(1).broadcast_to([128, 128, 8])
                S.op("gpsimd", lambda e, dT3=dT3, A3=A3: e.tensor_tensor(
                         out=dA3, in0=dT3, in1=A3, op=ALU.mult),
                     reads=[rr("dT_w"), rr(f"A_sb{dt}")], writes=[rr("dA_buf")])
                S.op("scalar", lambda e: e.activation(out=dA_buf[:, :],
                                                      in_=dA_buf[:, :], func=AF.Exp),
                     reads=[rr("dA_buf")], writes=[rr("dA_buf")])
                yield
                du3 = du_w[:, :].unsqueeze(2).broadcast_to([128, 128, 8])
                B3 = B3f[:, :, n0:n0 + 8]
                S.op("gpsimd", lambda e, du3=du3, B3=B3: e.tensor_tensor(
                         out=h3, in0=du3, in1=B3, op=ALU.mult),
                     reads=[rr("du_w"), rr("B_rep")], writes=[rr("h_buf")])
                yield
                for n in range(8):
                    S.op("vector", lambda e, n=n, dt=dt, n0=n0: e.tensor_tensor_scan(
                             out=h3[:, :, n], data0=dA3[:, :, n], data1=h3[:, :, n],
                             initial=carry[dt][:, n0 + n:n0 + n + 1],
                             op0=ALU.mult, op1=ALU.add),
                         reads=[rr("dA_buf"), rr("h_buf"), rr(f"carry{dt}")],
                         writes=[rr("h_buf")])
                S.op("vector", lambda e, dt=dt, n0=n0: e.tensor_copy(
                         out=carry[dt][:, n0:n0 + 8], in_=h_buf[:, 1016:1024]),
                     reads=[rr("h_buf")], writes=[rr(f"carry{dt}")])
                yield
                C3 = C3f[:, :, n0:n0 + 8]
                S.op("gpsimd", lambda e, C3=C3: e.tensor_tensor(
                         out=dA3, in0=h3, in1=C3, op=ALU.mult),
                     reads=[rr("h_buf"), rr("C_rep")], writes=[rr("dA_buf")])
                yield
                ytgt = (y_h[:, :], rr("y_h")) if hf == 0 else (du_w[:, :], rr("du_w"))
                S.op("vector", lambda e, t=ytgt[0]: e.tensor_reduce(
                         out=t, in_=dA3, axis=AX.X, op=ALU.add),
                     reads=[rr("dA_buf")], writes=[ytgt[1]])
                yield
            S.op("vector", lambda e: e.tensor_tensor(out=y_h[:, :],
                                                     in0=y_h[:, :],
                                                     in1=du_w[:, :],
                                                     op=ALU.add),
                 reads=[rr("y_h"), rr("du_w")], writes=[rr("y_h")])
            S.op("vector", lambda e, dt=dt: e.scalar_tensor_tensor(
                     out=y_h[:, :], in0=uT_w[:, :], scalar=D_sb[dt][:, 0:1],
                     in1=y_h[:, :], op0=ALU.mult, op1=ALU.add),
                 reads=[rr("uT_w"), rr(f"D_sb{dt}"), rr("y_h")], writes=[rr("y_h")])
            r0 = 256 * (w // 16) + 128 * dt
            cc0 = 128 * (w % 16)
            S.dma("qs", lambda e, r0=r0, cc0=cc0: e.dma_start(
                      out=ag_in[r0:r0 + 128, cc0:cc0 + 128],
                      in_=y_h[:, :]),
                  reads=[rr("y_h")], writes=[yw_res[w]])
            yield

    def emit_load_gu(e_, fc, par):
        S.dma("qw", lambda e, e_=e_, fc=fc, par=par: e.dma_start(
                  out=wg_bf[par][:, :], in_=ghi_in[e_, fc, :, :]),
              reads=[rr("ext_in")], writes=[rr(f"wg_bf{par}")])
        S.dma("qw", lambda e, e_=e_, fc=fc, par=par: e.dma_start(
                  out=wu_bf[par][:, :], in_=uhi_in[e_, fc, :, :]),
              reads=[rr("ext_in")], writes=[rr(f"wu_bf{par}")])

    def emit_load_wd(e_):
        for f in range(NFC):
            S.dma("qw", lambda e, e_=e_, f=f: e.dma_start(
                      out=wd_bf[f][:, :], in_=dhi_in[e_, f, :, :]),
                  reads=[rr("ext_in")], writes=[rr(f"wd_bf{f}")])

    def emit_ffn_gu(m, e_, fc, par):
        if e_ < NEXP and fc == 0:
            S.dma("qs", lambda e, e_=e_, m=m: e.dma_start(
                      out=c_rep[:, :],
                      in_=combT_dram[e_:e_ + 1, MTOK * m:MTOK * (m + 1)]
                          .broadcast_to([128, MTOK])),
                  reads=[rr("combT_dram")], writes=[rr("c_rep")])
        gres, ures = rr(f"wg_bf{par}"), rr(f"wu_bf{par}")
        for tc in range(2):
            for k in range(KT):
                S.op("tensor", lambda e, k=k, tc=tc, par=par: e.matmul(
                         pg[:, 512 * tc:512 * (tc + 1)],
                         lhsT=wg_bf[par][:, 128 * k:128 * (k + 1)],
                         rhs=xnT_own[k][:, 512 * tc:512 * (tc + 1)],
                         start=(k == 0), stop=(k == KT - 1)),
                     reads=[gres, rr(f"xnT_own{k}")], writes=[rr("pg")],
                     inc=(k == KT - 1))
            for k in range(KT):
                S.op("tensor", lambda e, k=k, tc=tc, par=par: e.matmul(
                         pu[:, 512 * tc:512 * (tc + 1)],
                         lhsT=wu_bf[par][:, 128 * k:128 * (k + 1)],
                         rhs=xnT_own[k][:, 512 * tc:512 * (tc + 1)],
                         start=(k == 0), stop=(k == KT - 1)),
                     reads=[ures, rr(f"xnT_own{k}")], writes=[rr("pu")],
                     inc=(k == KT - 1))
        S.op("scalar", lambda e: e.activation(out=silu_g[:, :], in_=pg[:, :],
                                              func=AF.Silu),
             reads=[rr("pg")], writes=[rr("silu_g")])
        ares = rr(f"act_f{fc}")
        if e_ < NEXP:
            S.op("vector", lambda e, fc=fc: e.tensor_tensor(
                     out=act_f[fc][:, :], in0=pu[:, :], in1=c_rep[:, :],
                     op=ALU.mult),
                 reads=[rr("pu"), rr("c_rep")], writes=[ares])
            S.op("vector", lambda e, fc=fc: e.tensor_tensor(
                     out=act_f[fc][:, :], in0=act_f[fc][:, :], in1=silu_g[:, :],
                     op=ALU.mult),
                 reads=[ares, rr("silu_g")], writes=[ares])
        else:
            S.op("vector", lambda e, fc=fc: e.tensor_tensor(
                     out=act_f[fc][:, :], in0=pu[:, :], in1=silu_g[:, :],
                     op=ALU.mult),
                 reads=[rr("pu"), rr("silu_g")], writes=[ares])

    def emit_ffn_routed(m, e_):
        prbufs = [(pr[:, 0:512], rr("pr")), (pr[:, 512:1024], rr("pr_hi")),
                  (prx[:, :], rr("prx"))]
        ri = [0]
        for dc in range(KT):
            for tc in range(2):
                pap, pres = prbufs[ri[0] % 3]
                ri[0] += 1
                for f in range(NFC):
                    S.op("tensor", lambda e, dc=dc, tc=tc, f=f, pap=pap: e.matmul(
                             pap,
                             lhsT=wd_bf[f][:, 128 * dc:128 * (dc + 1)],
                             rhs=act_f[f][:, 512 * tc:512 * (tc + 1)],
                             start=(f == 0), stop=(f == NFC - 1)),
                         reads=[rr(f"wd_bf{f}"), rr(f"act_f{f}")], writes=[pres],
                         inc=(f == NFC - 1))
                S.op("vector", lambda e, dc=dc, tc=tc, pap=pap: e.tensor_tensor(
                         out=acc[dc][:, 512 * tc:512 * (tc + 1)], in0=pap,
                         in1=acc[dc][:, 512 * tc:512 * (tc + 1)], op=ALU.add),
                     reads=[pres, rr(f"acc{dc}")], writes=[rr(f"acc{dc}")])

    def emit_macro_switch():
        for dc in range(KT):
            S.dma("qv", lambda e, dc=dc: e.dma_start(
                      out=accA_dram[128 * dc:128 * (dc + 1), :], in_=acc[dc][:, :]),
                  reads=[rr(f"acc{dc}")], writes=[rr("accA")])
            S.op("vector", lambda e, dc=dc: e.memset(acc[dc][:, :], 0.0),
                 reads=[], writes=[rr(f"acc{dc}")])
        for k in range(KT):
            S.dma("qx", lambda e, k=k: e.dma_start(
                      out=xnT_own[k][:, :],
                      in_=xnTB_dram[128 * k:128 * (k + 1), :]),
                  reads=[rr("xnTB")], writes=[rr(f"xnT_own{k}")])

    def emit_allgather():
        from concourse.aot_env import allow_ici_all_to_all

        def fn(e):
            with allow_ici_all_to_all():
                return e.collective_compute(
                    "AllToAll", ALU.bypass, replica_groups=[list(range(NCORE))],
                    ins=[ag_in], outs=[ag_out])
        S.op("gpsimd", fn, reads=yw_res, writes=[rr("ag_out")])

    def emit_assembly(j):
        m = 0 if j < 8 else 1
        jj = j % 8
        for dh in range(2):
            S.dma("qx", lambda e, j=j, dh=dh: e.dma_start(
                      out=x_half[:, :],
                      in_=x_own[128 * j:128 * (j + 1), 1024 * dh:1024 * (dh + 1)]),
                  reads=[rr("ext_in")], writes=[rr("x_half")])
            for dcl in range(8):
                dc = 8 * dh + dcl
                par = dc % 2
                if m == 0:
                    S.dma("qv", lambda e, dc=dc, jj=jj, par=par: e.dma_start(
                              out=acc_in[par],
                              in_=accA_dram[128 * dc:128 * (dc + 1),
                                            128 * jj:128 * (jj + 1)]),
                          reads=[rr("accA")], writes=[rr(f"acc_in{par}")])
                    src_ap = acc_in[par]
                    src_res = rr(f"acc_in{par}")
                else:
                    src_ap = acc[dc][:, 128 * jj:128 * (jj + 1)]
                    src_res = rr(f"acc{dc}")
                pslc = pm[:, 256 + 128 * par:256 + 128 * (par + 1)]
                S.op("tensor", lambda e, src_ap=src_ap, p=pslc: e.transpose(
                         p, in_=src_ap, identity=sid_f32[:, :]),
                     reads=[src_res, rr("sid_f32")], writes=[rr("pm")])
                S.op("vector", lambda e, dcl=dcl, p=pslc: e.scalar_tensor_tensor(
                         out=s_out[:, 128 * dcl:128 * (dcl + 1)], in0=p, scalar=1.0,
                         in1=x_half[:, 128 * dcl:128 * (dcl + 1)],
                         op0=ALU.mult, op1=ALU.add),
                     reads=[rr("pm"), rr("x_half")], writes=[rr("dA_buf")])
                S.dma("qx", lambda e, dc=dc, j=j, par=par: e.dma_start(
                          out=y_in[par],
                          in_=ag_out[128 * dc:128 * (dc + 1),
                                     128 * j:128 * (j + 1)]),
                      reads=[rr("ag_out")], writes=[rr(f"y_in{par}")])
                S.op("tensor", lambda e, par=par, p=pslc: e.transpose(
                         p, in_=y_in[par], identity=sid_f32[:, :]),
                     reads=[rr(f"y_in{par}"), rr("sid_f32")], writes=[rr("pm")])
                S.op("vector", lambda e, dcl=dcl, p=pslc: e.tensor_tensor(
                         out=s_out[:, 128 * dcl:128 * (dcl + 1)], in0=p,
                         in1=s_out[:, 128 * dcl:128 * (dcl + 1)], op=ALU.add),
                     reads=[rr("pm"), rr("dA_buf")], writes=[rr("dA_buf")])
            S.dma("qx", lambda e, j=j, dh=dh: e.dma_start(
                      out=out_own[128 * j:128 * (j + 1), 1024 * dh:1024 * (dh + 1)],
                      in_=s_out[:, :]),
                  reads=[rr("dA_buf")], writes=[rr("out_dram")])

    emit_startup()
    for j in range(OWNT):
        emit_miniprep(j)
    for dc in range(KT):
        S.op("vector", lambda e, dc=dc: e.memset(acc[dc][:, :], 0.0),
             reads=[], writes=[rr(f"acc{dc}")])

    prep_iters = []
    prep_next = [0]
    prep_done = [0]
    scan_iters = []
    scan_next = [0]

    def step_prep(nphase):
        used = 0
        for k in range(nphase):
            if len(prep_iters) < 2 and prep_next[0] < NT and \
                    (not prep_iters or prep_iters[-1][1] >= 2):
                prep_iters.append([emit_prep(prep_next[0]), 0])
                prep_next[0] += 1
            if not prep_iters:
                break
            it = prep_iters[k % len(prep_iters)]
            try:
                next(it[0])
                it[1] += 1
            except StopIteration:
                prep_iters.remove(it)
                prep_done[0] += 1
            used += 1
        return nphase - used

    def step_scan(nphase):
        for k in range(nphase):
            while len(scan_iters) < 1 and scan_next[0] < NT \
                    and scan_next[0] < prep_done[0]:
                scan_iters.append(emit_scan_gen(scan_next[0]))
                scan_next[0] += 1
            if not scan_iters:
                break
            it = scan_iters[k % len(scan_iters)]
            try:
                next(it)
            except StopIteration:
                scan_iters.remove(it)

    a2a_state = [0]
    asm_q = list(range(8))

    def maybe_tail():
        if a2a_state[0] == 0 and scan_next[0] == NT and not scan_iters \
                and not prep_iters and prep_next[0] == NT:
            emit_allgather()
            a2a_state[0] = 1
        elif a2a_state[0] == 1 and asm_q:
            emit_assembly(asm_q.pop(0))

    emit_load_gu(0, 0, 0)
    for m in range(2):
        for e_ in range(NE_ALL):
            emit_load_wd(e_)
            for fc in range(NFC):
                par = (e_ * NFC + fc) % 2
                nxt = (e_ * NFC + fc + 1) % 2
                emit_ffn_gu(m, e_, fc, par)
                if fc < NFC - 1:
                    emit_load_gu(e_, fc + 1, nxt)
                elif e_ < NE_ALL - 1:
                    emit_load_gu(e_ + 1, 0, nxt)
                elif m == 0:
                    emit_load_gu(0, 0, nxt)
                left = step_prep(3)
                step_scan(6 + left)
            emit_ffn_routed(m, e_)
            left = step_prep(3)
            step_scan(14 + left)
            if m == 1:
                maybe_tail()
        if m == 0:
            emit_macro_switch()
    step_prep(100000)
    step_scan(100000)
    if a2a_state[0] == 0:
        emit_allgather()
    for j in asm_q:
        emit_assembly(j)
    for j in range(8, OWNT):
        emit_assembly(j)

    toks = []
    for es in list(S.eng.values()) + list(S.queues.values()):
        for i, sem in enumerate(es.sems):
            if es.counts[i]:
                toks.append((sem, es.counts[i]))
    S.wait_tokens("sync", toks)

    with nc.Block() as block:
        S.emit(block)

    n_inst = sum(len(e.items) for e in S.eng.values())
    stack.close()
    return nc, n_inst


_PROG = {}


def kernel(L0_attn_x, L0_attn_A_log, L0_attn_D, L0_attn_x_proj_w, L0_attn_dt_proj_w,
           L0_attn_dt_proj_b, L0_ffn_gate_w, L0_ffn_wg, L0_ffn_wu, L0_ffn_wd,
           L0_ffn_sg, L0_ffn_su, L0_ffn_sd):
    x = np.ascontiguousarray(np.asarray(L0_attn_x, dtype=np.float32).reshape(TT, D))
    A_log = np.asarray(L0_attn_A_log, dtype=np.float32)
    Dv = np.asarray(L0_attn_D, dtype=np.float32).reshape(D, 1)
    xpw = np.asarray(L0_attn_x_proj_w, dtype=np.float32)
    dtw = np.asarray(L0_attn_dt_proj_w, dtype=np.float32)
    dtb = np.asarray(L0_attn_dt_proj_b, dtype=np.float32)
    gw = np.ascontiguousarray(np.asarray(L0_ffn_gate_w, dtype=np.float32))
    wg_ = np.asarray(L0_ffn_wg, dtype=np.float32)
    wu_ = np.asarray(L0_ffn_wu, dtype=np.float32)
    wd_ = np.asarray(L0_ffn_wd, dtype=np.float32)
    sg_ = np.asarray(L0_ffn_sg, dtype=np.float32)
    su_ = np.asarray(L0_ffn_su, dtype=np.float32)
    sd_ = np.asarray(L0_ffn_sd, dtype=np.float32)

    def gu_planes(w32, sh32):
        allw = np.concatenate(
            [w32, sh32.reshape(D, 2, FEXP).transpose(1, 0, 2)], axis=0)
        t = allw.reshape(NE_ALL, KT, 128, NFC, 128).transpose(0, 3, 2, 1, 4)
        v = np.ascontiguousarray(t).view('<u2').reshape(
            NE_ALL, NFC, 128, KT, 128, 2)
        hi = np.ascontiguousarray(v[..., 1]).reshape(NE_ALL, NFC, 128, D)
        lo = np.ascontiguousarray(v[..., 0]).reshape(NE_ALL, NFC, 128, D)
        return hi.view(ml_dtypes.bfloat16), lo.view(ml_dtypes.bfloat16)

    def d_planes(w32, sh32):
        allw = np.concatenate([w32, sh32.reshape(2, FEXP, D)], axis=0)
        t = allw.reshape(NE_ALL, NFC, 128, D)
        v = np.ascontiguousarray(t).view('<u2').reshape(NE_ALL, NFC, 128, D, 2)
        hi = np.ascontiguousarray(v[..., 1])
        lo = np.ascontiguousarray(v[..., 0])
        return hi.view(ml_dtypes.bfloat16), lo.view(ml_dtypes.bfloat16)

    ghi_h, glo_h = gu_planes(wg_, sg_)
    uhi_h, ulo_h = gu_planes(wu_, su_)
    dhi_h, dlo_h = d_planes(wd_, sd_)

    xpwT_h = np.ascontiguousarray(xpw.T)
    id_f32_h = np.eye(128, dtype=np.float32)

    if "nc" not in _PROG:
        _PROG["nc"], _PROG["n"] = build_program()
    nc = _PROG["nc"]

    in_maps = []
    for c in range(NCORE):
        dsl = slice(DSL * c, DSL * (c + 1))
        in_maps.append(dict(
            x_all=x,
            x_sl=np.ascontiguousarray(x[:, dsl]),
            x_own=np.ascontiguousarray(x[2048 * c:2048 * (c + 1), :]),
            alog_sl=np.ascontiguousarray(A_log[dsl, :]),
            d_sl=np.ascontiguousarray(Dv[dsl, :]),
            xpwT=xpwT_h,
            dtwT_sl=np.ascontiguousarray(dtw[dsl, :].T),
            bias_sl=np.ascontiguousarray(dtb[dsl].reshape(DSL, 1)),
            gate_w=gw,
            ghi=ghi_h, uhi=uhi_h, dhi=dhi_h,
            glo=glo_h, ulo=ulo_h, dlo=dlo_h,
            id_f32=id_f32_h,
        ))

    res = run_bass_kernel_spmd(nc, in_maps, core_ids=list(range(NCORE)))
    out = np.concatenate([res.results[c]["out_own"] for c in range(NCORE)], axis=0)
    return out.reshape(2, 8192, D)


if __name__ == "__main__":
    nc_, n_ = build_program()
    print("built", n_, "instructions")
